# version 08
# baseline (speedup 1.0000x reference)
import contextlib
import numpy as np
import ml_dtypes
import concourse.bass as bass
import concourse.mybir as mybir
from concourse.bass_utils import run_bass_kernel_spmd

F32 = mybir.dt.float32
BF16 = mybir.dt.bfloat16
AF = mybir.ActivationFunctionType
ALU = mybir.AluOpType
AX = mybir.AxisListType
BFNP = ml_dtypes.bfloat16

NCORE = 8
B = 2
T = 2048
D = 2048
CONVD = 8192
ZD = 4096
NV, NK, DK, DV = 32, 16, 128, 128
KC = 4
C = 128
NCH = T // C
HD = 5632
HB = HD // 128
KD = D // 128
CB = CONVD // 128
HALFLN = 0.5 * float(np.log(DK))
NW = float(HD * D)

ENGS = ("tensor", "vector", "scalar", "gpsimd", "sync")


class Prog:

    def __init__(self, nc):
        self.nc = nc
        self.q = {e: [] for e in ENGS}
        self.cnt = {}
        self.sem = {}

    def new_sem(self, stack, name):
        self.sem[name] = stack.enter_context(self.nc.semaphore(name))
        self.cnt[name] = 0
        return name

    def emit(self, eng, fn, waits=(), inc=None, inc_amt=1):
        if inc is not None:
            self.cnt[inc] += inc_amt
        self.q[eng].append((fn, tuple(waits), inc, inc_amt))

    def tok(self, sem):
        return (sem, self.cnt[sem])

    def fence(self, sems=("pe", "dve", "act", "gps", "dw", "dg", "ds")):
        toks = tuple(self.tok(s) for s in sems if s in self.cnt)
        for e in ENGS:
            self.q[e].append((lambda eng: None, toks, None, 1))

    def barrier(self, name="bar"):
        self.fence()
        if name not in self.sem:
            raise KeyError(name)
        target = self.cnt[name] + len(ENGS)
        for e in ENGS:
            self.cnt[name] += 1
            self.q[e].append((lambda eng: eng.drain(), (), name, 1))
        for e in ENGS:
            self.q[e].append((lambda eng: None, ((name, target),), None, 1))

    def run(self, block):
        prog = self

        def mk(engname):
            def body(eng):
                last = {}
                for fn, waits, inc, amt in prog.q[engname]:
                    for s, v in waits:
                        if v > 0 and last.get(s, -1) < v:
                            eng.wait_ge(prog.sem[s], v)
                            last[s] = v
                    ins = fn(eng)
                    if inc is not None:
                        assert ins is not None
                        ins.then_inc(prog.sem[inc], amt)
            return body

        block.tensor(mk("tensor"))
        block.vector(mk("vector"))
        block.scalar(mk("scalar"))
        block.gpsimd(mk("gpsimd"))
        block.sync(mk("sync"))


def mk_helpers(P):
    def mm(out, lhsT, rhs, start, stop, waits=(), inc=None):
        P.emit("tensor",
               lambda e, o=out, l=lhsT, r=rhs, s=start, t=stop:
               e.matmul(o, l, r, start=s, stop=t), waits, inc)

    def ptrans(out, in_, ident, waits=(), inc=None):
        P.emit("tensor",
               lambda e, o=out, i=in_, d=ident: e.transpose(o, i, d),
               waits, inc)

    def dve(opname, *args, waits=(), inc=None, **kw):
        P.emit("vector",
               lambda e, a=args, k=kw, n=opname: getattr(e, n)(*a, **k),
               waits, inc)

    def gps(opname, *args, waits=(), inc=None, **kw):
        P.emit("gpsimd",
               lambda e, a=args, k=kw, n=opname: getattr(e, n)(*a, **k),
               waits, inc)

    def act(out, in_, func, waits=(), inc=None, **kw):
        P.emit("scalar",
               lambda e, o=out, i=in_, f=func, k=kw:
               e.activation(o, i, f, **k), waits, inc)

    def dma(queue, out, in_, waits=(), inc=None, slow=False):
        P.emit(queue,
               lambda e, o=out, i=in_, s=slow:
               e.dma_start(out=o, in_=i, allow_slow_non_contiguous=s)
               if s else e.dma_start(out=o, in_=i),
               waits, inc, inc_amt=16)

    def dmat(out, in_, waits=(), inc=None):
        P.emit("sync",
               lambda e, o=out, i=in_: e.dma_start_transpose(o, i),
               waits, inc, inc_amt=16)


    return mm, ptrans, dve, gps, act, dma, dmat


def build_nc(debug=()):
    stack = contextlib.ExitStack()
    nc = bass.Bass("TRN2", target_bir_lowering=False)
    nc._exit_stack = stack

    def dram(name, shape, dtype=F32, kind="ExternalInput"):
        if kind == "Internal" and name in debug:
            kind = "ExternalOutput"
        return nc.dram_tensor(name, shape, dtype, kind=kind).ap()

    io = {}
    io["x"] = dram("x", [B, T, D])
    io["xT"] = dram("xT", [B, D, T])
    io["wqkv"] = dram("wqkv", [CB, 128, KD, 128])
    io["wz"] = dram("wz", [8, KD, 128, 512])
    io["wab"] = dram("wab", [KD, 128, 64])
    io["wout"] = dram("wout", [NV * DV, D])
    io["convw"] = dram("convw", [128, CB * KC])
    io["alog"] = dram("alog", [NV, 1])
    io["dtb"] = dram("dtb", [NV, 1])
    io["normw4"] = dram("normw4", [1, 512])
    io["w1t"] = dram("w1t", [HB, 128, KD, 128])
    io["w2t"] = dram("w2t", [HB, 128, KD, 128])
    io["w3t"] = dram("w3t", [4, HB, 128, 512])
    io["identf"] = dram("identf", [128, 128])
    io["identb"] = dram("identb", [128, 128], BF16)
    io["maskAu4"] = dram("maskAu4", [128, 512])
    io["maskM4"] = dram("maskM4", [128, 512])
    io["onesb"] = dram("onesb", [128, 1], BF16)
    io["onesf"] = dram("onesf", [1, 128])
    io["onescf"] = dram("onescf", [128, 1])
    io["cpool"] = dram("cpool", [128, 8])
    io["out"] = dram("out", [B, T, D], kind="ExternalOutput")

    sp = {}
    sp["qkvT"] = dram("sp_qkvT", [B, CB, 128, T], BF16, kind="Internal")
    sp["z"] = dram("sp_z", [B, NCH, 8, 128, 512], BF16, kind="Internal")
    sp["rows"] = dram("sp_rows", [B, 320, T], F32, kind="Internal")
    sp["exgc"] = dram("sp_exgc", [B, NV, NCH], F32, kind="Internal")
    sp["ogT"] = dram("sp_ogT", [B, NCH, 16, 128, 256], BF16, kind="Internal")
    sp["ogRAW"] = dram("sp_ogRAW", [B, NCH, 128, 4096], BF16,
                       kind="Internal")
    sp["o32d"] = dram("sp_o32d", [B, NCH, 8, 128, 512], F32,
                      kind="Internal")
    sp["ubfd"] = dram("sp_ubfd", [B, NCH, 8, 128, 512], BF16,
                      kind="Internal")
    sp["rcd"] = dram("sp_rcd", [B, NCH, 8, 128, 8], F32, kind="Internal")
    sp["y1"] = dram("sp_y1", [B, NCH, 128, D], F32, kind="Internal")
    sp["y1T"] = dram("sp_y1T", [B, NCH, KD, 128, 128], BF16, kind="Internal")
    sp["hid"] = dram("sp_hid", [HB, 8, 128, 512], BF16, kind="Internal")
    sp["w3q"] = dram("sp_w3q", [4, HB, 128, 512], BF16, kind="Internal")
    sp["tiny"] = dram("sp_tiny", [64, 64], F32, kind="Internal")
    sp["p3d"] = dram("sp_p3d", [4, 128, 2048], F32, kind="Internal")
    sp["w1qd"] = dram("sp_w1qd", [HB, 2, 128, 2048], BF16, kind="Internal")

    P = Prog(nc)
    for s in ["dw", "dg", "ds", "dx", "pe", "dve", "act", "gps", "bar",
              "fin", "scl", "xt", "wq", "wz", "wab", "spq0", "spq1",
              "spz0", "spz1", "lns0", "lns1", "abb", "rwb", "rwq", "rin",
              "exs", "exg", "qk0", "qk1", "v0", "v1", "z0", "z1", "s1a",
              "s1b", "s2l", "gb0", "gb1", "ogt", "ogs", "y1s", "y1ts",
              "hds", "w3s", "xr", "ogl", "w13", "w3l", "hdl", "y1l",
              "y1tl", "w3ql", "s2b"]:
        P.new_sem(stack, s)

    build_phases(nc, P, io, sp, stack, debug)

    blk = stack.enter_context(nc.Block())
    P.run(blk)
    stack.close()
    return nc, io, sp


def build_phases(nc, P, io, sp, ostack, debug):
    mm, ptrans, dve, gps, act, dma, dmat = mk_helpers(P)

    def sb_in(st, name, shape, dtype=F32):
        return st.enter_context(nc.sbuf_tensor("s_" + name, shape, dtype))

    def ps_in(st, name, shape, dtype=F32):
        return st.enter_context(nc.psum_tensor("p_" + name, shape, dtype))

    sbc = lambda n, s, d=F32: sb_in(ostack, n, s, d)
    identf = sbc("identf", [128, 128])
    identb = sbc("identb", [128, 128], BF16)
    maskAu4 = sbc("maskAu4", [128, 512])
    maskM4 = sbc("maskM4", [128, 512])
    dumd = sbc("dumd", [1, 8])
    duma = sbc("duma", [1, 8])
    onesb = sbc("onesb", [128, 1], BF16)
    onesf = sbc("onesf", [1, 128])
    onescf = sbc("onescf", [128, 1])
    convw = sbc("convw", [128, CB * KC])
    cpool = sbc("cpool", [128, 8])
    alog = sbc("alog", [NV, 1])
    dtb = sbc("dtb", [NV, 1])
    normwb = sbc("normwb", [128, 512])
    nexpalog = sbc("nexpalog", [NV, 1])
    gacc = sbc("gacc", [128, 4])

    for (t_, s_) in [(identf, io["identf"]), (identb, io["identb"]),
                     (onesb, io["onesb"]), (onesf, io["onesf"]),
                     (onescf, io["onescf"]), (convw, io["convw"]),
                     (cpool, io["cpool"]), (maskAu4, io["maskAu4"]),
                     (maskM4, io["maskM4"]),
                     (alog, io["alog"]), (dtb, io["dtb"])]:
        dma("sync", t_[:], s_[:], inc="dw")
    p0tok = P.tok("dw")
    nw4 = sbc("nw4", [1, 512])
    p0st = contextlib.ExitStack()
    pnw = ps_in(p0st, "pnw", [128, 512])
    dma("sync", nw4[:], io["normw4"][:], inc="dw")
    t0 = P.tok("dw")
    mm(pnw[:], onesf[:, :], nw4[:, :], True, True,
       waits=[p0tok, t0], inc="pe")
    dve("tensor_copy", normwb[:], pnw[:], waits=[P.tok("pe")], inc="dve")
    expalog = sbc("expalog", [NV, 1])
    act(expalog[:], alog[:], AF.Exp, waits=[p0tok], inc="act")
    dve("tensor_scalar", nexpalog[:], expalog[:], -1.0, 0.0, ALU.mult,
        ALU.add, waits=[P.tok("act")], inc="dve")

    mid = contextlib.ExitStack()
    ostack.enter_context(mid)
    exgcf = [sb_in(mid, f"exgcf{b}", [1, NV * NCH]) for b in range(B)]

    P.barrier()
    p0st.close()
    env = dict(locals())
    phase_p1(nc, P, io, sp, ostack, mid, debug, env)
    P.barrier()
    phase_p2(nc, P, io, sp, ostack, mid, debug, env)
    P.barrier()
    mid.close()
    phase_p3(nc, P, io, sp, ostack, debug, env)
    P.barrier()
    phase_p4(nc, P, io, sp, ostack, debug, env)

    fin_tok = P.tok("dx")
    for e in ENGS:
        P.q[e].append((lambda eng: None, (fin_tok,), None, 1))


def phase_p1(nc, P, io, sp, ostack, mid, debug, env):
    mm, ptrans, dve, gps, act, dma, dmat = mk_helpers(P)
    sb_in = env["sb_in"]
    ps_in = env["ps_in"]
    onesb, onesf, convw = env["onesb"], env["onesf"], env["convw"]
    identf = env["identf"]
    alog, dtb, nexpalog = env["alog"], env["dtb"], env["nexpalog"]
    exgcf = env["exgcf"]

    for b in range(B):
        bst = contextlib.ExitStack()
        xT = sb_in(bst, f"xT{b}", [128, KD * T], BF16)
        rowsland = sb_in(bst, f"rowsland{b}", [64, T])
        dma("gpsimd", xT[:].rearrange("p (kd t) -> p kd t", kd=KD),
            io["xT"][b].rearrange("(kd p) t -> p kd t", p=128),
            inc="xt")
        xT_tok = P.tok("xt")

        st = contextlib.ExitStack()
        wbuf = sb_in(st, f"wbuf{b}", [128, KD * 128])
        wb16 = [sb_in(st, f"wb16_{b}{i}", [128, KD * 128], BF16)
                for i in range(2)]
        mixrow = [sb_in(st, f"mixrow{b}{i}", [128, T + KC - 1])
                  for i in range(2)]
        convrow = [sb_in(st, f"convrow{b}{i}", [128, T]) for i in range(2)]
        qkvrow = [sb_in(st, f"qkvrow{b}{i}", [128, T], BF16)
                  for i in range(2)]
        sqrow = [sb_in(st, f"sqrow{b}{i}", [128, T], BF16) for i in range(2)]
        lnbuf = [sb_in(st, f"lnbuf{b}{i}", [1, T]) for i in range(2)]
        pq = [ps_in(st, f"pq{b}{i}", [128, 1024]) for i in range(2)]
        pss = ps_in(st, f"pss{b}", [1, 1024])

        for i in range(2):
            dve("memset", mixrow[i][:, 0:KC - 1], 0.0, inc="dve")

        conv_done = [(None, 0), (None, 0)]
        w_done = ("pe", 0)
        cast_done = [("pe", 0), ("pe", 0)]
        qkv_done = [(None, 0), (None, 0)]
        sq_done = [(None, 0), (None, 0)]
        ln_done = [(None, 0), (None, 0)]
        pq_done = [("dve", 0), ("dve", 0)]

        for cb in range(CB):
            pr = cb % 2
            wts = [w_done] if w_done[1] else []
            dma("sync", wbuf[:],
                io["wqkv"][cb].rearrange("p kd c -> p (kd c)"),
                waits=wts, inc="wq")
            cwts = [cast_done[pr]] if cast_done[pr][1] else []
            dve("tensor_copy", wb16[pr][:], wbuf[:],
                waits=cwts + [P.tok("wq")], inc="dve")
            cast_tok = P.tok("dve")
            mm_toks = []
            for tg in range(2):
                for kd in range(KD):
                    for tt in range(2):
                        tb = tg * 2 + tt
                        w8 = [cast_tok, xT_tok]
                        if kd == 0 and tt == 0 and pq_done[tg][1]:
                            w8.append(pq_done[tg])
                        mm(pq[tg][:, tt * 512:(tt + 1) * 512],
                           wb16[pr][:, kd * 128:(kd + 1) * 128],
                           xT[:, kd * T + tb * 512:kd * T + (tb + 1) * 512],
                           kd == 0, kd == KD - 1,
                           waits=w8, inc="pe" if kd == KD - 1 else None)
                mm_toks.append(P.tok("pe"))
            w_done = cast_tok
            cast_done[pr] = mm_toks[-1]
            for tg in range(2):
                dve("tensor_copy",
                    mixrow[pr][:, KC - 1 + tg * 1024:KC - 1 + (tg + 1) * 1024],
                    pq[tg][:], waits=[mm_toks[tg]], inc="dve")
                pq_done[tg] = P.tok("dve")
            evac_tok = P.tok("dve")
            cwt = [conv_done[pr]] if conv_done[pr][1] else []
            dve("tensor_scalar", convrow[pr][:],
                mixrow[pr][:, KC - 1:KC - 1 + T],
                convw[:, cb * KC + 3:cb * KC + 4], 0.0, ALU.mult, ALU.add,
                waits=cwt + [evac_tok], inc="dve")
            for i in (2, 1, 0):
                dve("scalar_tensor_tensor", convrow[pr][:],
                    mixrow[pr][:, i:i + T],
                    convw[:, cb * KC + i:cb * KC + i + 1], convrow[pr][:],
                    ALU.mult, ALU.add, inc="dve")
            dve("tensor_copy", env["dumd"][0:1, 0:1],
                convrow[pr][0:1, 0:1], inc="dve")
            conv_tok = P.tok("dve")
            qwt = [qkv_done[pr]] if qkv_done[pr][1] else []
            act(qkvrow[pr][:], convrow[pr][:], AF.Silu,
                waits=qwt + [conv_tok], inc="act")
            conv_done[pr] = P.tok("act")
            silu_tok = P.tok("act")
            dma("gpsimd", sp["qkvT"][b, cb], qkvrow[pr][:],
                waits=[silu_tok], inc="spq0" if pr == 0 else "spq1")
            qkv_done[pr] = P.tok("spq0" if pr == 0 else "spq1")
            if cb < 32:
                swt = [sq_done[pr]] if sq_done[pr][1] else []
                act(sqrow[pr][:], qkvrow[pr][:], AF.Square,
                    waits=swt + [silu_tok], inc="act")
                sq_tok = P.tok("act")
                lwt = [ln_done[pr]] if ln_done[pr][1] else []
                for tb in range(4):
                    mw = [sq_tok]
                    if tb >= 2:
                        mw.append(P.tok("act"))
                    mm(pss[:, (tb % 2) * 512:(tb % 2) * 512 + 512],
                       onesb[:, :],
                       sqrow[pr][:, tb * 512:(tb + 1) * 512],
                       True, True, waits=mw, inc="pe")
                    act(lnbuf[pr][:, tb * 512:(tb + 1) * 512],
                        pss[:, (tb % 2) * 512:(tb % 2) * 512 + 512],
                        AF.Ln, bias=env["cpool"][0:1, 0:1],
                        waits=lwt + [P.tok("pe")], inc="act")
                sq_done[pr] = P.tok("pe")
                ln_tok = P.tok("act")
                rowi = (240 + cb) if cb < 16 else (224 + cb - 16)
                dma("gpsimd", sp["rows"][b, rowi], lnbuf[pr][:],
                    waits=[ln_tok], inc="lns0" if pr == 0 else "lns1")
                ln_done[pr] = P.tok("lns0" if pr == 0 else "lns1")
        st.close()
        P.fence()

        st = contextlib.ExitStack()
        wzbuf = sb_in(st, f"wzbuf{b}", [128, KD * 512])
        wzb16 = [sb_in(st, f"wzb16_{b}{i}", [128, KD * 512], BF16)
                 for i in range(2)]
        wabb = sb_in(st, f"wabb{b}", [128, KD * 64], BF16)
        zrow = [sb_in(st, f"zrow{b}{i}", [128, 512], BF16) for i in range(2)]
        abf = sb_in(st, f"abf{b}", [128, 64])
        pz = [ps_in(st, f"pz{b}{i}", [128, 512]) for i in range(2)]
        pab = ps_in(st, f"pab{b}", [128, 64])
        pabt = ps_in(st, f"pabt{b}", [64, 128])

        dma("gpsimd", wabb[:].rearrange("p (kd a) -> p kd a", kd=KD),
            io["wab"].rearrange("kd p a -> p kd a"), inc="wab")
        wab_tok = P.tok("wab")
        zw_done = ("pe", 0)
        zcast_done = [("pe", 0), ("pe", 0)]
        zrow_done = [(None, 0), (None, 0)]
        pz_done = [(None, 0), (None, 0)]
        ab_evac = (None, 0)
        abt_evac = (None, 0)
        for zcb in range(8):
            zpr = zcb % 2
            zwt = [zw_done] if zw_done[1] else []
            dma("sync", wzbuf[:].rearrange("p (kd c) -> p kd c", kd=KD),
                io["wz"][zcb].rearrange("kd p c -> p kd c"),
                waits=zwt, inc="wz")
            zct = [zcast_done[zpr]] if zcast_done[zpr][1] else []
            dve("tensor_copy", wzb16[zpr][:], wzbuf[:],
                waits=zct + [P.tok("wz")], inc="dve")
            zcast = P.tok("dve")
            last_mm = ("pe", 0)
            for tb in range(16):
                tpr = tb % 2
                for kd in range(KD):
                    w8 = [zcast, xT_tok]
                    if kd == 0 and pz_done[tpr][1]:
                        w8.append(pz_done[tpr])
                    mm(pz[tpr][:],
                       xT[:, kd * T + tb * 128:kd * T + tb * 128 + 128],
                       wzb16[zpr][:, kd * 512:(kd + 1) * 512],
                       kd == 0, kd == KD - 1,
                       waits=w8, inc="pe" if kd == KD - 1 else None)
                zmm = P.tok("pe")
                last_mm = zmm
                if zcb == 0:
                    abw = [ab_evac] if ab_evac[1] else []
                    for kd in range(KD):
                        mm(pab[:],
                           xT[:, kd * T + tb * 128:kd * T + tb * 128 + 128],
                           wabb[:, kd * 64:(kd + 1) * 64],
                           kd == 0, kd == KD - 1,
                           waits=[wab_tok, xT_tok] + (abw if kd == 0 else []),
                           inc="pe" if kd == KD - 1 else None)
                    abmm = P.tok("pe")
                    aew = [abt_evac] if abt_evac[1] else []
                    dve("tensor_copy", abf[:], pab[:],
                        waits=[abmm] + aew, inc="dve")
                    ab_evac = P.tok("dve")
                    ptrans(pabt[:], abf[:], identf[:],
                           waits=[ab_evac], inc="pe")
                    abt_mm = P.tok("pe")
                    dve("tensor_copy", rowsland[:, tb * 128:(tb + 1) * 128],
                        pabt[:], waits=[abt_mm], inc="dve")
                    abt_evac = P.tok("dve")
                zewt = [zrow_done[tpr]] if zrow_done[tpr][1] else []
                dve("tensor_copy", zrow[tpr][:], pz[tpr][:],
                    waits=zewt + [zmm], inc="dve")
                pz_done[tpr] = P.tok("dve")
                dma("gpsimd", sp["z"][b, tb, zcb], zrow[tpr][:],
                    waits=[P.tok("dve")],
                    inc="spz0" if tpr == 0 else "spz1")
                zrow_done[tpr] = P.tok("spz0" if tpr == 0 else "spz1")
            zw_done = zcast
            zcast_done[zpr] = last_mm
        ab_all = P.tok("dve")
        dma("gpsimd", sp["rows"][b, 256:288, :], rowsland[0:32, :],
            waits=[ab_all], inc="abb")
        dma("gpsimd", sp["rows"][b, 288:320, :], rowsland[32:64, :],
            waits=[ab_all], inc="abb")
        st.close()
        P.fence()

        st = contextlib.ExitStack()
        rows_p1(nc, P, io, sp, st, b, env)
        st.close()
        bst.close()
        P.fence()

    for b in range(B):
        dma("sync", exgcf[b][0:1, :].rearrange("o (c h) -> o c h", h=NV),
            sp["exgc"][b].transpose([1, 0]).unsqueeze(0),
            waits=[P.tok("exs")], inc="exg", slow=True)
    env["flats_tok"] = P.tok("exg")


def rows_p1(nc, P, io, sp, st, b, env):
    mm, ptrans, dve, gps, act, dma, dmat = mk_helpers(P)
    sb_in, ps_in = env["sb_in"], env["ps_in"]
    dtb, nexpalog = env["dtb"], env["nexpalog"]
    dumd, duma = env["dumd"], env["duma"]

    def fl_dve_r(srct):
        dve("tensor_copy", dumd[0:1, 0:1], srct[0:1, 0:1], inc="dve")

    def fl_act_r(srct):
        act(duma[0:1, 0:1], srct[0:1, 0:1], AF.Copy, inc="act")

    apre = sb_in(st, f"apre{b}", [32, T])
    bpre = sb_in(st, f"bpre{b}", [32, T])
    tA = sb_in(st, f"tA{b}", [32, T])
    tB = sb_in(st, f"tB{b}", [32, T])
    grow = sb_in(st, f"grow{b}", [32, T])
    G = sb_in(st, f"G{b}", [32, T])
    lnb = sb_in(st, f"lnb{b}", [32, T])
    lkd = sb_in(st, f"lkd{b}", [32, T])
    lqd = sb_in(st, f"lqd{b}", [32, T])
    xq = sb_in(st, f"xq{b}", [32, T])
    cpp = [sb_in(st, f"cpp{i}_{b}", [32, 192]) for i in range(2)]
    gcg = sb_in(st, f"gcg{b}", [32, NCH])

    lw_tok = [P.tok("abb"), P.tok("lns0"), P.tok("lns1")]
    dma("sync", apre[:, :], sp["rows"][b, 256:288, :], waits=lw_tok,
        inc="rin")
    dma("sync", bpre[:, :], sp["rows"][b, 288:320, :], inc="rin")
    dma("sync", lkd[:, :],
        sp["rows"][b][224:240, :].unsqueeze(1).broadcast_to([16, 2, T]),
        inc="rin")
    dma("sync", lqd[:, :],
        sp["rows"][b][240:256, :].unsqueeze(1).broadcast_to([16, 2, T]),
        inc="rin")
    in_tok = P.tok("rin")

    act(tA[:, :], apre[:, :], AF.Abs, bias=dtb[0:32, :], waits=[in_tok],
        inc="act")
    act(tA[:, :], tA[:, :], AF.Exp, scale=-1.0, waits=[P.tok("act")],
        inc="act")
    act(tA[:, :], tA[:, :], AF.Ln, bias=1.0, waits=[P.tok("act")],
        inc="act")
    fl_act_r(tA)
    spa_tok = P.tok("act")
    dve("tensor_scalar", tB[:, :], apre[:, :], dtb[0:32, :], 0.0,
        ALU.add, ALU.max, waits=[in_tok], inc="dve")
    dve("tensor_add", tB[:, :], tB[:, :], tA[:, :], waits=[spa_tok],
        inc="dve")
    dve("tensor_scalar", grow[:, :], tB[:, :], nexpalog[0:32, :], 0.0,
        ALU.mult, ALU.add, inc="dve")
    fl_dve_r(grow)
    grow_tok = P.tok("dve")
    act(tA[:, :], bpre[:, :], AF.Abs, waits=[in_tok, grow_tok], inc="act")
    act(tA[:, :], tA[:, :], AF.Exp, scale=-1.0, waits=[P.tok("act")],
        inc="act")
    act(tA[:, :], tA[:, :], AF.Ln, bias=1.0, waits=[P.tok("act")],
        inc="act")
    fl_act_r(tA)
    spb_tok = P.tok("act")
    dve("tensor_scalar", tB[:, :], bpre[:, :], 0.0, None, ALU.min,
        waits=[spb_tok], inc="dve")
    dve("tensor_sub", lnb[:, :], tB[:, :], tA[:, :], inc="dve")
    lnb_tok = P.tok("dve")
    act(tB[:, :], bpre[:, :], AF.Sigmoid, waits=[lnb_tok], inc="act")
    fl_act_r(tB)
    dma("gpsimd", sp["rows"][b, 192:224, :], tB[:, :],
        waits=[P.tok("act")], inc="rwb")

    for i in range(2):
        dve("memset", cpp[i][:, 0:64], 0.0, inc="dve")
    pad_tok = P.tok("dve")
    for c in range(NCH):
        dve("tensor_copy", cpp[0][:, 64:192], grow[:, c * 128:(c + 1) * 128],
            waits=[grow_tok, pad_tok], inc="dve")
        srci = 0
        off = 1
        while off < 128:
            dve("tensor_add", cpp[1 - srci][:, 64:192], cpp[srci][:, 64:192],
                cpp[srci][:, 64 - off:192 - off], inc="dve")
            srci = 1 - srci
            off *= 2
        dve("tensor_copy", G[:, c * 128:(c + 1) * 128], cpp[srci][:, 64:192],
            inc="dve")
    fl_dve_r(G)
    G_tok = P.tok("dve")

    act(gcg[:, :], G[:, 127::128], AF.Exp, waits=[G_tok], inc="act")
    fl_act_r(gcg)
    dma("gpsimd", sp["exgc"][b], gcg[:, :], waits=[P.tok("act")],
        inc="exs")

    dve("tensor_scalar", tA[:, :], lkd[:, :], -0.5, 0.0, ALU.mult, ALU.add,
        waits=[in_tok, G_tok], inc="dve")
    dve("tensor_add", xq[:, :], G[:, :], tA[:, :], inc="dve")
    dve("tensor_add", xq[:, :], xq[:, :], lnb[:, :], inc="dve")
    fl_dve_r(xq)
    x1_tok = P.tok("dve")
    dma("gpsimd", sp["rows"][b, 0:32, :], xq[:, :], waits=[x1_tok],
        inc="rwq")
    act(tB[:, :], xq[:, :], AF.Exp, waits=[x1_tok], inc="act")
    fl_act_r(tB)
    dve("tensor_scalar", tB[:, :], tB[:, :], -1.0, 0.0, ALU.mult, ALU.add,
        waits=[P.tok("act")], inc="dve")
    fl_dve_r(tB)
    dma("gpsimd", sp["rows"][b, 96:128, :], tB[:, :], waits=[P.tok("dve")],
        inc="rwb")
    s0_dma = (P.tok("rwb"), P.tok("rwq"))
    dve("tensor_scalar", tA[:, :], lkd[:, :], 0.5, 0.0, ALU.mult, ALU.add,
        waits=list(s0_dma), inc="dve")
    dve("tensor_add", xq[:, :], G[:, :], tA[:, :], inc="dve")
    dve("tensor_scalar", xq[:, :], xq[:, :], -1.0, 0.0, ALU.mult, ALU.add,
        inc="dve")
    fl_dve_r(xq)
    x2_tok = P.tok("dve")
    dma("gpsimd", sp["rows"][b, 32:64, :], xq[:, :], waits=[x2_tok],
        inc="rwq")
    x2_dma = P.tok("rwq")
    dve("tensor_scalar", tA[:, :], lqd[:, :], -0.5, -HALFLN, ALU.mult,
        ALU.add, waits=[x2_dma], inc="dve")
    dve("tensor_add", xq[:, :], G[:, :], tA[:, :], inc="dve")
    fl_dve_r(xq)
    x3_tok = P.tok("dve")
    dma("gpsimd", sp["rows"][b, 64:96, :], xq[:, :], waits=[x3_tok],
        inc="rwq")
    act(tB[:, :], xq[:, :], AF.Exp, waits=[x3_tok], inc="act")
    fl_act_r(tB)
    dma("gpsimd", sp["rows"][b, 128:160, :], tB[:, :], waits=[P.tok("act")],
        inc="rwb")
    s1_dma = P.tok("rwb")
    for c in range(NCH):
        sl = slice(c * 128, (c + 1) * 128)
        dve("tensor_scalar", tA[:, sl], G[:, sl],
            G[:, c * 128 + 127:c * 128 + 128], -1.0, ALU.subtract,
            ALU.mult, waits=[s1_dma] if c == 0 else [], inc="dve")
    dve("scalar_tensor_tensor", tA[:, :], lkd[:, :], -0.5, tA[:, :],
        ALU.mult, ALU.add, inc="dve")
    fl_dve_r(tA)
    act(tB[:, :], tA[:, :], AF.Exp, waits=[P.tok("dve")], inc="act")
    fl_act_r(tB)
    dma("gpsimd", sp["rows"][b, 160:192, :], tB[:, :], waits=[P.tok("act")],
        inc="rwb")


def phase_p2(nc, P, io, sp, ostack, mid, debug, env):
    mm, ptrans, dve, gps, act, dma, dmat = mk_helpers(P)
    sb_in, ps_in = env["sb_in"], env["ps_in"]
    onesf, identf, identb = env["onesf"], env["identf"], env["identb"]
    maskAu4, maskM4 = env["maskAu4"], env["maskM4"]
    normwb, cpool = env["normwb"], env["cpool"]
    exgcf = env["exgcf"]
    flats_tok = env["flats_tok"]
    dumd, duma = env["dumd"], env["duma"]

    st = contextlib.ExitStack()
    S32 = sb_in(st, "S32", [128, 64 * 128])
    Sbf = sb_in(st, "Sbf", [128, 64 * 128], BF16)
    qsl = [sb_in(st, f"qsl{i}", [128, 16 * 128], BF16) for i in range(2)]
    ksl = [sb_in(st, f"ksl{i}", [128, 16 * 128], BF16) for i in range(2)]
    vsl = [sb_in(st, f"vsl{i}", [128, 32 * 128], BF16) for i in range(2)]
    zsl = [sb_in(st, f"zsl{i}", [128, 4096], BF16) for i in range(2)]
    stg1 = [sb_in(st, f"stg1_{i}", [1, 1024]) for i in range(2)]
    stg2 = sb_in(st, "stg2", [128, 128])
    stg2b = sb_in(st, "stg2b", [32, 128])
    scxg = [sb_in(st, f"scxg{i}", [128, 192]) for i in range(2)]
    etmpD = sb_in(st, "etmpD", [128, 512])
    etmpM = sb_in(st, "etmpM", [128, 512])
    maskAu4b = sb_in(st, "maskAu4b", [128, 512], BF16)
    maskM4b = sb_in(st, "maskM4b", [128, 512], BF16)
    ktm = [sb_in(st, f"ktm{i}", [128, 256], BF16) for i in range(2)]
    atneg = [sb_in(st, f"atneg{i}", [128, 512], BF16) for i in range(2)]
    aneg = [sb_in(st, f"aneg{i}", [128, 512], BF16) for i in range(2)]
    mhat = [sb_in(st, f"mhat{i}", [128, 512], BF16) for i in range(2)]
    bvf = [sb_in(st, f"bvf{i}", [128, 512]) for i in range(2)]
    apow = [sb_in(st, f"apow{i}", [128, 512], BF16) for i in range(2)]
    atpow = [sb_in(st, f"atpow{i}", [128, 512], BF16) for i in range(2)]
    xbf = sb_in(st, "xbf", [128, 512], BF16)
    ubf = sb_in(st, "ubf", [128, 512], BF16)
    kdq = sb_in(st, "kdq", [128, 512], BF16)
    t1g = sb_in(st, "t1g", [128, 512])
    og32 = [sb_in(st, f"og32_{i}", [128, 4096], BF16) for i in range(2)]
    szg = sb_in(st, "szg", [128, 4096], BF16)
    sz2 = sb_in(st, "sz2", [128, 512], BF16)
    sqd = sb_in(st, "sqd", [128, 128])
    rsum = sb_in(st, "rsum", [128, 32])
    rstd = sb_in(st, "rstd", [128, 32])
    og = sb_in(st, "og", [128, 4096], BF16)
    ogT = sb_in(st, "ogT", [128, 4096], BF16)

    pA = ps_in(st, "pA", [128, 512])
    pB = ps_in(st, "pB", [128, 512])
    pX = ps_in(st, "pX", [128, 512])
    pP = ps_in(st, "pP", [128, 512])
    pM = ps_in(st, "pM", [128, 512])
    pD = ps_in(st, "pD", [128, 512])
    pK = ps_in(st, "pK", [128, 512])
    pT = ps_in(st, "pT", [128, 512], BF16)

    def fl_dve(src):
        dve("tensor_copy", dumd[0:1, 0:1], src[0:1, 0:1], inc="dve")

    def fl_act(src):
        act(duma[0:1, 0:1], src[0:1, 0:1], AF.Copy, inc="act")

    dumg = sb_in(st, "dumg", [1, 8])

    def fl_gps(src):
        gps("tensor_copy", dumg[0:1, 0:1], src[0:1, 0:1], inc="gps")

    dve("memset", S32[:], 0.0, inc="dve")
    dve("memset", Sbf[:], 0.0, inc="dve")
    dve("tensor_copy", maskAu4b[:], maskAu4[:], inc="dve")
    dve("tensor_copy", maskM4b[:], maskM4[:], inc="dve")
    fl_dve(maskM4b)
    init_tok = P.tok("dve")

    gacc = env["gacc"]
    gtmp = sb_in(st, "gtmp", [128, 1])
    gbuf = [sb_in(st, f"gbuf{i}", [128, 512]) for i in range(2)]
    dve("memset", gacc[:], 0.0, inc="dve")
    gamma_jobs = []
    for wi, wname in enumerate(("w1t", "w2t")):
        for hb in range(HB):
            for half in range(4):
                gamma_jobs.append(
                    (wi, io[wname][hb].rearrange("p kd c -> p (kd c)")
                     [:, half * 512:(half + 1) * 512], 512))
    for db in range(4):
        for hb in range(HB):
            gamma_jobs.append((2, io["w3t"][db, hb], 512))
    gj = {"i": 0, "buf": [(None, 0), (None, 0)], "red": (None, 0)}

    def gamma_step(njobs=18):
        if "no_gamma" in debug:
            return
        for _ in range(njobs):
            i = gj["i"]
            if i >= len(gamma_jobs):
                return
            gj["i"] = i + 1
            wi, apw, ncols = gamma_jobs[i]
            pr = i % 2
            w8 = [gj["buf"][pr]] if gj["buf"][pr][1] else []
            gbs = "gb0" if pr == 0 else "gb1"
            dma("scalar", gbuf[pr][:, 0:ncols], apw, waits=w8, inc=gbs)
            dve("tensor_reduce", gtmp[:], gbuf[pr][:, 0:ncols],
                AX.X, ALU.add, True,
                waits=[P.tok(gbs)] + w(gj["red"]), inc="dve")
            gj["buf"][pr] = P.tok("dve")
            gps("tensor_add", gacc[:, wi:wi + 1], gacc[:, wi:wi + 1],
                gtmp[:], waits=[P.tok("dve")], inc="gps")
            gj["red"] = P.tok("gps")

    env["gamma_step"] = gamma_step

    NT = (None, 0)

    def w(*toks):
        return [t for t in toks if t is not None and t[1]]

    tk = {
        "qk_free": [NT, NT], "v_free": [NT, NT], "z_free": [NT, NT],
        "qk_tok": [NT, NT], "v_tok": [NT, NT], "z_tok": [NT, NT],
        "stg1_free": [NT, NT], "stg1_tok": [NT, NT],
        "rbD_free": [NT, NT], "rbM_free": [NT, NT],
        "dsD_free": NT, "dsM_free": NT,
        "stg2_free": NT, "stg2b_free": NT,
        "scxg_free": [NT, NT], "scxg_tok": [NT, NT],
        "pD_free": NT, "pM_free": NT, "pK_free": NT, "pT_free": NT,
        "pP_free": NT, "pX_free": NT, "pA_free": NT, "pB_free": NT,
        "etD_free": NT, "etM_free": NT, "t1g_free": NT,
        "atneg_free": [NT, NT], "aneg_free": [NT, NT],
        "mhat_free": [NT, NT], "bvf_free": [NT, NT], "ktm_free": [NT, NT],
        "og32_free": [NT, NT], "og_free": NT, "ogT_free": NT,
        "rsum_free": NT, "rstd_free": NT, "szg_free": NT,
        "epi_o": NT, "silu_t": NT, "lnexp_t": NT, "og_done": NT,
        "sq_all": NT, "sz2_free": NT,
    }
    tk["pD_free"] = init_tok
    tk["pM_free"] = init_tok
    flq = []

    def fill(n=1):
        for _ in range(n):
            if flq:
                flq.pop(0)()
    state_tok = [[init_tok] * 8 for _ in range(B)]
    bt = {}

    def load_qkv_slabs(m):
        if m >= 2 * NCH:
            return
        b, c = m % 2, m // 2
        qs = "qk0" if b == 0 else "qk1"
        dma("sync", qsl[b][:].rearrange("p (h t) -> p h t", h=16),
            sp["qkvT"][b, 0:16, :, c * 128:(c + 1) * 128]
            .transpose([1, 0, 2]), waits=w(tk["qk_free"][b]), inc=qs)
        dma("sync", ksl[b][:].rearrange("p (h t) -> p h t", h=16),
            sp["qkvT"][b, 16:32, :, c * 128:(c + 1) * 128]
            .transpose([1, 0, 2]), inc=qs)
        tk["qk_tok"][b] = P.tok(qs)
        vs = "v0" if b == 0 else "v1"
        dma("sync", vsl[b][:].rearrange("p (h t) -> p h t", h=32),
            sp["qkvT"][b, 32:64, :, c * 128:(c + 1) * 128]
            .transpose([1, 0, 2]), waits=w(tk["v_free"][b]), inc=vs)
        tk["v_tok"][b] = P.tok(vs)

    def load_z_slab(m):
        if m >= 2 * NCH:
            return
        b, c = m % 2, m // 2
        zs = "z0" if b == 0 else "z1"
        dma("sync", zsl[b][:].rearrange("p (zc j) -> p zc j", zc=8),
            sp["z"][b, c].transpose([1, 0, 2]),
            waits=w(tk["z_free"][b]), inc=zs)
        tk["z_tok"][b] = P.tok(zs)

    def prep(m):
        if m >= 2 * NCH:
            return
        b, c, sx = m % 2, m // 2, m % 2
        csl = slice(c * 128, (c + 1) * 128)
        dma("sync", stg2[:], sp["rows"][b, 96:224, csl],
            waits=w(tk["stg2_free"]), inc="s2l")
        dma("sync", stg2b[:], sp["rows"][b, 32:64, csl],
            waits=w(tk["stg2b_free"]), inc="s2b")
        ptrans(pD[:, 0:128], stg2[:], identf[:],
               waits=[P.tok("s2l")] + w(tk["pD_free"]), inc="pe")
        ptrans(pD[:, 128:160], stg2b[:], identf[0:32, 0:32],
               waits=[P.tok("s2b")], inc="pe")
        mm(pD[:, 160:192], onesf[:, :],
           exgcf[b][0:1, c * 32:(c + 1) * 32], True, True,
           waits=[flats_tok], inc="pe")
        pe_t = P.tok("pe")
        dve("tensor_copy", scxg[sx][:], pD[:, 0:192],
            waits=[pe_t] + w(tk["scxg_free"][sx]), inc="dve")
        fl_dve(scxg[sx])
        tok = P.tok("dve")
        tk["scxg_tok"][sx] = tok
        tk["stg2_free"] = tok
        tk["stg2b_free"] = tok
        tk["pD_free"] = tok

    PART_ORDER = ("masks", "kk", "expd", "expm", "vtrans_bvf",
                  "mhat", "atneg_aneg", "ktm")

    def build_parts(m, g):
        b, c, gp = m % 2, m // 2, g % 2
        sx = m % 2
        u0 = 4 * g
        sc = scxg[sx]
        csl = slice(c * 128, (c + 1) * 128)
        s1s = "s1a" if gp == 0 else "s1b"
        ent = bt.setdefault(g, {})

        def p_masks():
            dma("sync", stg1[gp][0:1, 0:512]
                .rearrange("o (u j) -> o u j", j=128),
                sp["rows"][b][u0:u0 + 4, csl].unsqueeze(0),
                waits=w(tk["stg1_free"][gp]), inc=s1s)
            dma("sync", stg1[gp][0:1, 512:1024]
                .rearrange("o (u j) -> o u j", j=128),
                sp["rows"][b][64 + u0:64 + u0 + 4, csl].unsqueeze(0),
                inc=s1s)
            s1t = P.tok(s1s)
            mm(pD[:], onesf[:, :], stg1[gp][0:1, 0:512], True, False,
               waits=[s1t] + w(tk["pD_free"]), inc=None)
            mm(pD[:], identb[:], maskAu4b[:], False, True, inc="pe")
            mm(pM[:], onesf[:, :], stg1[gp][0:1, 512:1024], True, False,
               waits=w(tk["pM_free"]), inc=None)
            mm(pM[:], identb[:], maskM4b[:], False, True, inc="pe")
            ent["rm"] = P.tok("pe")
            tk["stg1_free"][gp] = ent["rm"]

        def p_kk():
            for pi in range(2):
                hs = 2 * g + pi
                kT = ksl[b][:, hs * 128:hs * 128 + 128]
                qT = qsl[b][:, hs * 128:hs * 128 + 128]
                mm(pK[:, pi * 256:pi * 256 + 128], kT, kT, True, True,
                   waits=([tk["qk_tok"][b]] + w(tk["pK_free"]))
                   if pi == 0 else [], inc=None)
                mm(pK[:, pi * 256 + 128:pi * 256 + 256], kT, qT, True,
                   True, inc="pe" if pi == 1 else None)
            ent["kk"] = P.tok("pe")

        def p_expd():
            for du in range(4):
                u = u0 + du
                sl = slice(du * 128, (du + 1) * 128)
                act(etmpD[:, sl], pD[:, sl], AF.Exp,
                    bias=sc[:, 128 + u:129 + u],
                    waits=([ent["rm"], tk["scxg_tok"][sx]]
                           + w(tk["etD_free"])) if du == 0 else [],
                    inc="act")
            fl_act(etmpD)
            ent["expd"] = P.tok("act")
            tk["pD_free"] = ent["expd"]

        def p_expm():
            for du in range(4):
                u = u0 + du
                sl = slice(du * 128, (du + 1) * 128)
                act(etmpM[:, sl], pM[:, sl], AF.Exp,
                    bias=sc[:, 128 + u:129 + u],
                    waits=([ent["rm"], tk["scxg_tok"][sx]]
                           + w(tk["etM_free"])) if du == 0 else [],
                    inc="act")
            fl_act(etmpM)
            ent["expm"] = P.tok("act")
            tk["pM_free"] = ent["expm"]

        def p_vtrans_bvf():
            for du in range(4):
                u = u0 + du
                ptrans(pT[:, du * 128:(du + 1) * 128],
                       vsl[b][:, u * 128:(u + 1) * 128], identb[:],
                       waits=([tk["v_tok"][b]] + w(tk["pT_free"]))
                       if du == 0 else [], inc="pe" if du == 3 else None)
            vt_mm = P.tok("pe")
            for du in range(4):
                u = u0 + du
                sl = slice(du * 128, (du + 1) * 128)
                act(bvf[gp][:, sl], pT[:, sl], AF.Copy,
                    scale=sc[:, 96 + u:97 + u],
                    waits=([vt_mm, tk["scxg_tok"][sx]]
                           + w(tk["bvf_free"][gp]))
                    if du == 0 else [], inc="act")
            fl_act(bvf[gp])
            ent["bv"] = P.tok("act")

        def p_atneg_aneg():
            for du in range(4):
                hpi = du // 2
                sl = slice(du * 128, (du + 1) * 128)
                dve("scalar_tensor_tensor", atneg[gp][:, sl],
                    pK[:, hpi * 256:hpi * 256 + 128], -1.0, etmpD[:, sl],
                    ALU.mult, ALU.mult,
                    waits=([ent["kk"], ent["expd"]]
                           + w(tk["atneg_free"][gp]))
                    if du == 0 else [], inc="dve")
            fl_dve(atneg[gp])
            ent["at"] = P.tok("dve")
            tk["etD_free"] = ent["at"]
            tk["pK_free"] = ent["at"]
            for du in range(4):
                sl = slice(du * 128, (du + 1) * 128)
                ptrans(pT[:, sl], atneg[gp][:, sl], identb[:],
                       waits=[ent["at"], ent["bv"]] if du == 0 else [],
                       inc="pe" if du == 3 else None)
            an_mm = P.tok("pe")
            dve("tensor_copy", aneg[gp][:], pT[:],
                waits=[an_mm] + w(tk["aneg_free"][gp]), inc="dve")
            fl_dve(aneg[gp])
            ent["an"] = P.tok("dve")

        def p_mhat():
            for du in range(4):
                hpi = du // 2
                sl = slice(du * 128, (du + 1) * 128)
                dve("tensor_mul", mhat[gp][:, sl], etmpM[:, sl],
                    pK[:, hpi * 256 + 128:hpi * 256 + 256],
                    waits=([ent["expm"], ent["kk"]]
                           + w(tk["mhat_free"][gp]))
                    if du == 0 else [], inc="dve")
            fl_dve(mhat[gp])
            ent["mh"] = P.tok("dve")
            tk["etM_free"] = ent["mh"]

        def p_ktm():
            for pi in range(2):
                hs = 2 * g + pi
                ptrans(pT[:, pi * 128:(pi + 1) * 128],
                       ksl[b][:, hs * 128:hs * 128 + 128], identb[:],
                       waits=[ent["an"], tk["qk_tok"][b]]
                       if pi == 0 else [], inc="pe" if pi == 1 else None)
            ktm_mm = P.tok("pe")
            dve("tensor_copy", ktm[gp][:], pT[:, 0:256],
                waits=[ktm_mm] + w(tk["ktm_free"][gp]), inc="dve")
            fl_dve(ktm[gp])
            ent["ktm"] = P.tok("dve")
            tk["pT_free"] = ent["ktm"]

        return dict(masks=p_masks, kk=p_kk, expd=p_expd, expm=p_expm,
                    vtrans_bvf=p_vtrans_bvf, atneg_aneg=p_atneg_aneg,
                    mhat=p_mhat, ktm=p_ktm)

    def solve(m, g, parts=None):
        b, c, gp = m % 2, m // 2, g % 2
        sx = m % 2
        u0 = 4 * g
        sc = scxg[sx]
        bg = bt.pop(g)
        sw = state_tok[b][g]
        for pi in range(2):
            hs = 2 * g + pi
            kT = ksl[b][:, hs * 128:hs * 128 + 128]
            mw = ([sw, tk["qk_tok"][b]] + w(tk["pP_free"])) \
                if pi == 0 else []
            mm(pP[:, pi * 256:(pi + 1) * 256], kT,
               Sbf[:, (b * 32 + u0 + 2 * pi) * 128:
                   (b * 32 + u0 + 2 * pi + 2) * 128],
               True, True, waits=mw, inc="pe" if pi == 1 else None)
        p_mm = P.tok("pe")
        if parts:
            parts["masks"]()
        for du in range(4):
            u = u0 + du
            sl = slice(du * 128, (du + 1) * 128)
            dve("scalar_tensor_tensor", t1g[:, sl], pP[:, sl],
                sc[:, u:u + 1], bvf[gp][:, sl], ALU.mult, ALU.add,
                waits=([p_mm, bg["bv"], tk["scxg_tok"][sx]]
                       + w(tk["t1g_free"])) if du == 0 else [], inc="dve")
        fl_dve(t1g)
        x_tok = P.tok("dve")
        tk["bvf_free"][gp] = x_tok
        if parts:
            parts["vtrans_bvf"]()
        acur, atcur = aneg[gp], atneg[gp]
        pow_evac = (bg["an"], bg["at"])
        for j in range(7):
            if j == 0:
                dve("tensor_copy", xbf[:], t1g[:], waits=[x_tok],
                    inc="dve")
                fl_dve(xbf)
                snap = P.tok("dve")
            elif j % 2 == 1:
                act(xbf[:], pX[:], AF.Copy, waits=[x_tok], inc="act")
                fl_act(xbf)
                snap = P.tok("act")
            else:
                dve("tensor_copy", xbf[:], pX[:], waits=[x_tok], inc="dve")
                fl_dve(xbf)
                snap = P.tok("dve")
            if j < 6:
                half = j == 5
                sqw = [pow_evac[0], pow_evac[1]]
                if j == 0:
                    sqw += w(tk["pA_free"], tk["pB_free"])
                for du in range(4):
                    sl = slice(du * 128, (du + 1) * 128)
                    if not half:
                        mm(pA[:, sl], atcur[:, sl], acur[:, sl], True,
                           True, waits=sqw if du == 0 else [], inc=None)
                    mm(pB[:, sl], acur[:, sl], atcur[:, sl], True, True,
                       waits=sqw if (du == 0 and half) else [],
                       inc="pe" if du == 3 else None)
                sq_mm = P.tok("pe")
                if not half:
                    act(apow[j % 2][:], pA[:], AF.Copy, waits=[sq_mm],
                        inc="act")
                    fl_act(apow[j % 2])
                    ap_t = P.tok("act")
                    if j == 4:
                        tk["pA_free"] = ap_t
                dve("tensor_copy", atpow[j % 2][:], pB[:], waits=[sq_mm],
                    inc="dve")
                fl_dve(atpow[j % 2])
                atp_t = P.tok("dve")
                if half:
                    tk["pB_free"] = atp_t
                    next_evac = (atp_t, atp_t)
                else:
                    next_evac = (ap_t, atp_t)
            lw = [snap, pow_evac[0], pow_evac[1]]
            if j == 0:
                lw += w(tk["pX_free"])
            for du in range(4):
                sl = slice(du * 128, (du + 1) * 128)
                if j == 0:
                    mm(pX[:, sl], atcur[:, sl], xbf[:, sl], du == 0, False,
                       waits=lw if du == 0 else [], inc=None)
                    mm(pX[:, sl], identb[:], xbf[:, sl], False, True,
                       inc="pe" if du == 3 else None)
                else:
                    mm(pX[:, sl], atcur[:, sl], xbf[:, sl], False, True,
                       waits=lw if du == 0 else [],
                       inc="pe" if du == 3 else None)
            x_tok = P.tok("pe")
            if parts:
                if j == 1:
                    parts["kk"]()
                elif j == 3:
                    parts["expd"]()
                elif j == 5:
                    parts["expm"]()
            if j in (2, 6):
                fill()
            if j < 6:
                acur = None if half else apow[j % 2]
                atcur = atpow[j % 2]
                pow_evac = next_evac
        dve("tensor_copy", ubf[:], pX[:], waits=[x_tok], inc="dve")
        fl_dve(ubf)
        u_tok = P.tok("dve")
        tk["atneg_free"][gp] = x_tok
        tk["aneg_free"][gp] = x_tok
        for pi in range(2):
            hs = 2 * g + pi
            qT = qsl[b][:, hs * 128:hs * 128 + 128]
            mm(pP[:, pi * 256:(pi + 1) * 256], qT,
               Sbf[:, (b * 32 + u0 + 2 * pi) * 128:
                   (b * 32 + u0 + 2 * pi + 2) * 128],
               True, True, waits=[x_tok] if pi == 0 else [],
               inc="pe" if pi == 1 else None)
        qs_mm = P.tok("pe")
        for du in range(4):
            sl = slice(du * 128, (du + 1) * 128)
            mm(pX[:, sl], mhat[gp][:, sl], ubf[:, sl], True, True,
               waits=[u_tok, bg["mh"]] if du == 0 else [],
               inc="pe" if du == 3 else None)
        mu_mm = P.tok("pe")
        tk["mhat_free"][gp] = mu_mm
        if parts:
            parts["mhat"]()
        fill()
        for du in range(4):
            u = u0 + du
            sl = slice(du * 128, (du + 1) * 128)
            act(t1g[:, sl], pP[:, sl], AF.Copy,
                scale=sc[:, 32 + u:33 + u],
                waits=[qs_mm, u_tok] if du == 0 else [], inc="act")
        fl_act(t1g)
        co_t = P.tok("act")
        ph = m % 2
        ow = [co_t, mu_mm]
        if g == 0:
            ow += w(tk["og32_free"][ph])
        dve("tensor_add", og32[ph][:, g * 512:(g + 1) * 512], t1g[:],
            pX[:], waits=ow, inc="dve")
        fl_dve(og32[ph])
        o_t = P.tok("dve")
        tk["pX_free"] = o_t
        tk["t1g_free"] = o_t
        if parts:
            parts["atneg_aneg"]()
        for du in range(4):
            u = u0 + du
            hpi = du // 2
            sl = slice(du * 128, (du + 1) * 128)
            act(kdq[:, sl], ktm[gp][:, hpi * 128:hpi * 128 + 128],
                AF.Copy, scale=sc[:, 64 + u:65 + u],
                waits=[bg["ktm"], tk["scxg_tok"][sx]] if du == 0 else [],
                inc="act")
        fl_act(kdq)
        kd_t = P.tok("act")
        tk["ktm_free"][gp] = kd_t
        if parts:
            parts["ktm"]()
        for du in range(4):
            sl = slice(du * 128, (du + 1) * 128)
            mm(pP[:, sl], kdq[:, sl], ubf[:, sl], True, True,
               waits=[kd_t, u_tok, co_t] if du == 0 else [],
               inc="pe" if du == 3 else None)
        su_mm = P.tok("pe")
        for du in range(4):
            u = u0 + du
            usl = slice((b * 32 + u) * 128, (b * 32 + u + 1) * 128)
            sl = slice(du * 128, (du + 1) * 128)
            dve("scalar_tensor_tensor", S32[:, usl], S32[:, usl],
                sc[:, 160 + u:161 + u], pP[:, sl], ALU.mult, ALU.add,
                waits=[su_mm, tk["scxg_tok"][sx]] if du == 0 else [],
                inc="dve")
        gsl = slice((b * 32 + u0) * 128, (b * 32 + u0 + 4) * 128)
        s32_t = P.tok("dve")
        gps("tensor_copy", Sbf[:, gsl], S32[:, gsl], waits=[s32_t],
            inc="gps")
        fl_gps(Sbf)
        st_t = P.tok("gps")
        state_tok[b][g] = st_t
        tk["pP_free"] = st_t
        return o_t

    def epi_thunks(m):
        b, c, ph = m % 2, m // 2, m % 2
        ogf = og32[ph]

        def sq_piece(k):
            def f():
                fw = [tk["epi_o"]]
                if k == 0:
                    dve("memset", rsum[:], 0.0,
                        waits=w(tk["rsum_free"]), inc="dve")
                    fl_dve(rsum)
                    fw.append(P.tok("dve"))
                first = True
                for du in range(4):
                    u = k * 4 + du
                    sl = slice(u * 128, (u + 1) * 128)
                    act(sqd[:], ogf[:, sl], AF.Square,
                        accum_out=rsum[:, u:u + 1],
                        waits=fw if first else [], inc="act")
                    first = False
                if k == 7:
                    fl_act(rsum)
                    tk["sq_all"] = P.tok("act")
            return f

        def lnexp():
            act(rstd[:], rsum[:], AF.Ln, bias=cpool[:, 0:1],
                scale=1.0 / 128.0,
                waits=[tk["sq_all"]] + w(tk["rstd_free"]), inc="act")
            tk["rsum_free"] = P.tok("act")
            act(rstd[:], rstd[:], AF.Exp, scale=-0.5,
                waits=[P.tok("act")], inc="act")
            fl_act(rstd)
            tk["lnexp_t"] = P.tok("act")

        def silu_piece(k0):
            def f():
                zw = [tk["z_tok"][b]] + (w(tk["szg_free"])
                                         if k0 == 0 else [])
                for k in range(k0, k0 + 4):
                    act(szg[:, k * 512:(k + 1) * 512],
                        zsl[b][:, k * 512:(k + 1) * 512], AF.Silu,
                        waits=zw if k == k0 else [], inc="act")
                if k0 == 4:
                    fl_act(szg)
                    tk["z_free"][b] = P.tok("act")
                    tk["silu_t"] = P.tok("act")
            return f

        def mul_block(k0):
            def f():
                for k in (k0, k0 + 1):
                    mw = [tk["silu_t"], tk["lnexp_t"]] + w(tk["sz2_free"])
                    if k == k0 == 0:
                        mw += w(tk["og_free"])
                    gps("tensor_mul", sz2[:],
                        szg[:, k * 512:(k + 1) * 512], normwb[:],
                        waits=mw, inc="gps")
                    for du in range(4):
                        u = k * 4 + du
                        sl = slice(u * 128, (u + 1) * 128)
                        dve("scalar_tensor_tensor", og[:, sl],
                            ogf[:, sl], rstd[:, u:u + 1],
                            sz2[:, du * 128:(du + 1) * 128],
                            ALU.mult, ALU.mult,
                            waits=[P.tok("gps")] if du == 0 else [],
                            inc="dve")
                    tk["sz2_free"] = P.tok("dve")
                if k0 == 6:
                    fl_dve(og)
                    tok = P.tok("dve")
                    tk["og_done"] = tok
                    tk["og32_free"][ph] = tok
                    tk["rstd_free"] = tok
                    tk["szg_free"] = tok
            return f

        def ogspill():
            otw = [tk["og_done"]] + w(tk["ogT_free"])
            dmat(ogT[:].rearrange("p (u j) -> p u j", u=32),
                 og[:].rearrange("p (u j) -> p u j", u=32),
                 waits=otw, inc="ogt")
            tk["og_free"] = P.tok("ogt")
            dma("gpsimd", sp["ogT"][b, c].rearrange("u p j -> p u j"),
                ogT[:].rearrange("p (u j) -> p u j", u=32),
                waits=[tk["og_free"]], inc="ogs")
            tk["ogT_free"] = P.tok("ogs")

        def mul_last():
            mul_block(6)()
            ogspill()

        return ([sq_piece(k) for k in range(8)]
                + [lnexp, silu_piece(0), silu_piece(4),
                   mul_block(0), mul_block(2), mul_block(4), mul_last])

    pending = []
    load_qkv_slabs(0)
    load_z_slab(0)
    prep(0)
    for g0 in (0, 1):
        bp = build_parts(0, g0)
        for k in PART_ORDER:
            bp[k]()
    for m in range(2 * NCH):
        b = m % 2
        flq.extend(pending)
        pending = []
        for _ in range(6):
            flq.append(lambda: gamma_step(3))
        load_qkv_slabs(m + 1)
        last_o = None
        for g in range(8):
            if g < 6:
                parts = build_parts(m, g + 2)
            elif m + 1 < 2 * NCH:
                parts = build_parts(m + 1, g - 6)
            else:
                parts = None
            last_o = solve(m, g, parts)
            if g == 4:
                load_z_slab(m + 1)
            if g == 5:
                prep(m + 1)
        tk["qk_free"][b] = P.tok("pe")
        tk["v_free"][b] = P.tok("pe")
        tk["epi_o"] = last_o
        pending = epi_thunks(m)
    while flq:
        flq.pop(0)()
    for f in pending:
        f()
    while gj["i"] < len(gamma_jobs):
        gamma_step(20)
    fl_gps(gacc)
    env["gamma_red"] = P.tok("gps")
    st.close()


def phase_p3(nc, P, io, sp, ostack, debug, env):
    mm, ptrans, dve, gps, act, dma, dmat = mk_helpers(P)
    sb_in, ps_in = env["sb_in"], env["ps_in"]
    cpool = env["cpool"]
    dumd, duma = env["dumd"], env["duma"]
    gacc = env["gacc"]
    onesf, onescf = env["onesf"], env["onescf"]

    def fl_dve(src):
        dve("tensor_copy", dumd[0:1, 0:1], src[0:1, 0:1], inc="dve")

    def fl_act(src):
        act(duma[0:1, 0:1], src[0:1, 0:1], AF.Copy, inc="act")

    thrc = sb_in(ostack, "thrc", [128, 4])
    nthrc = sb_in(ostack, "nthrc", [128, 4])
    env["thrc"] = thrc
    env["nthrc"] = nthrc
    st = contextlib.ExitStack()
    wout = sb_in(st, "wout", [128, 32 * 2048], BF16)
    ogl = [sb_in(st, f"ogl{i}", [128, 4096], BF16) for i in range(2)]
    xld = [sb_in(st, f"xld{i}", [128, 2048]) for i in range(2)]
    y1pre = sb_in(st, "y1pre", [128, 2048])
    y1sb = sb_in(st, "y1sb", [128, 2048])
    y1b = sb_in(st, "y1b", [128, 2048], BF16)
    y1T = sb_in(st, "y1T", [128, 2048], BF16)
    cols = sb_in(st, "cols", [128, 8])
    w3buf = [sb_in(st, f"w3buf{i}", [128, 1024]) for i in range(2)]
    w3qb = [sb_in(st, f"w3qb{i}", [128, 1024], BF16) for i in range(2)]
    colsd = sb_in(st, "colsd", [128, 8])
    p3 = [ps_in(st, f"p3_{i}", [128, 512]) for i in range(2)]
    pg = ps_in(st, "pg", [128, 8])

    mm(pg[0:3, 0:1], gacc[:, 0:3], onescf[:, :], True, True,
       waits=[env["gamma_red"]], inc="pe")
    dve("tensor_copy", cols[0:3, 0:1], pg[0:3, 0:1], waits=[P.tok("pe")],
        inc="dve")
    dve("tensor_scalar", cols[0:3, 1:2], cols[0:3, 0:1],
        0.5 / NW, 0.5e-8, ALU.mult, ALU.add, waits=[P.tok("dve")],
        inc="dve")
    fl_dve(cols)
    dma("gpsimd", sp["tiny"][0:3, 0:1], cols[0:3, 1:2],
        waits=[P.tok("dve")], inc="w3s", slow=True)
    dma("sync", cols[0:1, 4:7], sp["tiny"][0:3, 0:1].transpose([1, 0]),
        waits=[P.tok("w3s")], inc="w3l", slow=True)
    mm(pg[:, 4:7], onesf[:, :], cols[0:1, 4:7], True, True,
       waits=[P.tok("w3l")], inc="pe")
    dve("tensor_copy", thrc[:, 0:3], pg[:, 4:7], waits=[P.tok("pe")],
        inc="dve")
    dve("tensor_scalar", nthrc[:, 0:3], thrc[:, 0:3], -1.0, 0.0,
        ALU.mult, ALU.add, waits=[P.tok("dve")], inc="dve")
    fl_dve(nthrc)
    env["thr_tok"] = P.tok("dve")
    if "sp_p3d" in debug:
        dma("gpsimd", sp["p3d"][2][:, 0:4], thrc[:], waits=[env["thr_tok"]],
            inc="dg")
        dma("gpsimd", sp["p3d"][2][:, 4:8], gacc[:], waits=[env["thr_tok"]],
            inc="dg")
        dma("gpsimd", sp["p3d"][3][:, 0:8], cols[:], waits=[env["thr_tok"]],
            inc="dg")

    w3jobs = []
    for db in range(4):
        for hb in range(0, HB, 2):
            w3jobs.append((db, hb))
    w3j = {"i": 0, "buf": [(None, 0), (None, 0)]}

    def w3_step(njobs=6):
        for _ in range(njobs):
            i = w3j["i"]
            if i >= len(w3jobs):
                return
            w3j["i"] = i + 1
            db, hb = w3jobs[i]
            pr = i % 2
            w8 = [w3j["buf"][pr]] if w3j["buf"][pr][1] else []
            sname = "gb0" if pr == 0 else "gb1"
            dma("scalar", w3buf[pr][:].rearrange("p (h c) -> p h c", h=2),
                io["w3t"][db, hb:hb + 2].transpose([1, 0, 2]),
                waits=w8, inc=sname)
            dve("tensor_scalar", w3qb[pr][:], w3buf[pr][:],
                nthrc[:, 2:3], 0.0, ALU.is_lt, ALU.add,
                waits=[P.tok(sname), env["thr_tok"]], inc="dve")
            dve("scalar_tensor_tensor", w3qb[pr][:], w3buf[pr][:],
                thrc[:, 2:3], w3qb[pr][:], ALU.is_gt, ALU.subtract,
                inc="dve")
            fl_dve(w3qb[pr])
            sspill = "w3ql" if pr == 0 else "hds"
            dma("gpsimd",
                sp["w3q"][db, hb:hb + 2].transpose([1, 0, 2]),
                w3qb[pr][:].rearrange("p (h c) -> p h c", h=2),
                waits=[P.tok("dve")], inc=sspill)
            w3j["buf"][pr] = P.tok(sspill)

    for i in range(4):
        dma("gpsimd",
            wout[:, i * 16384:(i + 1) * 16384]
            .rearrange("p (kc d) -> p kc d", kc=8),
            io["wout"][i * 1024:(i + 1) * 1024]
            .rearrange("(kc p) d -> p kc d", p=128), inc="w13")
    wout_tok = P.tok("w13")

    ogl_tok = [(None, 0), (None, 0)]
    x_tok = [(None, 0), (None, 0)]
    ogl_free = [(None, 0), (None, 0)]
    x_free = [(None, 0), (None, 0)]
    y1pre_free = (None, 0)
    y1sb_free = (None, 0)
    y1b_free = (None, 0)
    y1T_free = (None, 0)
    p3_free = [(None, 0), (None, 0)]

    def load_iter(b, c, pr):
        sname = "qk0" if pr == 0 else "qk1"
        w8 = [ogl_free[pr]] if ogl_free[pr][1] else []
        dma("sync", ogl[pr][:].rearrange("p (u j) -> p u j", u=16),
            sp["ogT"][b, c].transpose([1, 0, 2]), waits=w8, inc=sname)
        ogl_tok[pr] = P.tok(sname)
        sname = "v0" if pr == 0 else "v1"
        w8 = [x_free[pr]] if x_free[pr][1] else []
        dma("sync", xld[pr][:], io["x"][b, c * 128:(c + 1) * 128, :],
            waits=w8, inc=sname)
        x_tok[pr] = P.tok(sname)

    it = 0
    load_iter(0, 0, 0)
    for b in range(B):
        for c in range(NCH):
            pr = it % 2
            nb, nch = (b, c + 1) if c + 1 < NCH else (b + 1, 0)
            if nb < B:
                load_iter(nb, nch, 1 - pr)
            w3_step()
            for db in range(4):
                dpr = db % 2
                mw = [ogl_tok[pr], wout_tok]
                if p3_free[dpr][1]:
                    mw.append(p3_free[dpr])
                for kc in range(32):
                    mm(p3[dpr][:],
                       ogl[pr][:, kc * 128:(kc + 1) * 128],
                       wout[:, kc * 2048 + db * 512:
                            kc * 2048 + (db + 1) * 512],
                       kc == 0, kc == 31,
                       waits=mw if kc == 0 else [],
                       inc="pe" if kc == 31 else None)
                e_mm = P.tok("pe")
                ew = [e_mm, x_tok[pr]]
                if y1pre_free[1] and db == 0:
                    ew.append(y1pre_free)
                dve("tensor_add", y1pre[:, db * 512:(db + 1) * 512],
                    p3[dpr][:], xld[pr][:, db * 512:(db + 1) * 512],
                    waits=ew, inc="dve")
                p3_free[dpr] = P.tok("dve")
            fl_dve(y1pre)
            pre_tok = P.tok("dve")
            if "sp_p3d" in debug and b == 0 and c == 0:
                dma("gpsimd", sp["p3d"][0], y1pre[:], waits=[pre_tok],
                    inc="dg")
                dma("gpsimd", sp["p3d"][1], xld[pr][:],
                    waits=[x_tok[pr]], inc="dg")
                dma("gpsimd", sp["p3d"][2],
                    wout[:, 0:2048], waits=[wout_tok], inc="dg")
                dma("gpsimd", sp["p3d"][3], ogl[pr][:, 0:2048],
                    waits=[ogl_tok[pr]], inc="dg")
                P.fence()
            dve("tensor_reduce", cols[:, 0:1], y1pre[:], AX.X, ALU.add,
                waits=[pre_tok], inc="dve")
            fl_dve(cols)
            dve("memset", cols[:, 1:2], 0.0, inc="dve")
            fl_dve(cols)
            sw = [P.tok("dve"), pre_tok]
            if y1sb_free[1]:
                sw.append(y1sb_free)
            act(y1sb[:], y1pre[:], AF.Square, accum_out=cols[:, 1:2],
                waits=sw, inc="act")
            fl_act(cols)
            sq_tok = P.tok("act")
            dve("tensor_scalar", cols[:, 2:3], cols[:, 0:1], 1.0 / D, 0.0,
                ALU.mult, ALU.add, waits=[sq_tok], inc="dve")
            dve("tensor_scalar", cols[:, 3:4], cols[:, 1:2], 1.0 / D, 0.0,
                ALU.mult, ALU.add, inc="dve")
            dve("scalar_tensor_tensor", cols[:, 6:7], cols[:, 2:3],
                cols[:, 2:3], cols[:, 3:4], ALU.mult,
                ALU.subtract, waits=[P.tok("dve")], inc="dve")
            act(cols[:, 4:5], cols[:, 6:7], AF.Ln, scale=-1.0,
                bias=cpool[:, 3:4], waits=[P.tok("dve")], inc="act")
            act(cols[:, 4:5], cols[:, 4:5], AF.Exp, scale=-0.5,
                waits=[P.tok("act")], inc="act")
            fl_act(cols)
            if "sp_p3d" in debug and b == 0 and c == 0:
                dma("gpsimd", sp["p3d"][1][:, 0:8], cols[:],
                    waits=[P.tok("act")], inc="dg")
                P.fence()
            dve("tensor_scalar", y1sb[:], y1pre[:], cols[:, 2:3],
                cols[:, 4:5], ALU.subtract, ALU.mult,
                waits=[P.tok("act")], inc="dve")
            fl_dve(y1sb)
            y1_tok = P.tok("dve")
            if "sp_p3d" in debug and b == 0 and c == 0:
                dve("tensor_copy", colsd[:], cols[:], inc="dve")
                dma("gpsimd", sp["p3d"][1][:, 0:8], colsd[:],
                    waits=[P.tok("dve")], inc="dg")
            y1pre_free = y1_tok
            bw = [y1_tok]
            if y1b_free[1]:
                bw.append(y1b_free)
            dve("tensor_copy", y1b[:], y1sb[:], waits=bw, inc="dve")
            fl_dve(y1b)
            y1b_tok = P.tok("dve")
            dma("gpsimd", sp["y1"][b, c], y1sb[:], waits=[y1_tok],
                inc="y1s")
            y1sb_free = P.tok("y1s")
            tw = [y1b_tok]
            if y1T_free[1]:
                tw.append(y1T_free)
            dmat(y1T[:].rearrange("p (kd j) -> p kd j", kd=16),
                 y1b[:].rearrange("p (kd j) -> p kd j", kd=16),
                 waits=tw, inc="ogt")
            y1b_free = P.tok("ogt")
            dma("gpsimd", sp["y1T"][b, c].rearrange("kd p j -> p kd j"),
                y1T[:].rearrange("p (kd j) -> p kd j", kd=16),
                waits=[P.tok("ogt")], inc="y1ts")
            y1T_free = P.tok("y1ts")
            it += 1
    while w3j["i"] < len(w3jobs):
        w3_step(6)
    st.close()


def phase_p4(nc, P, io, sp, ostack, debug, env):
    mm, ptrans, dve, gps, act, dma, dmat = mk_helpers(P)
    sb_in, ps_in = env["sb_in"], env["ps_in"]
    cpool = env["cpool"]
    dumd, duma = env["dumd"], env["duma"]
    thrc, nthrc = env["thrc"], env["nthrc"]

    def fl_dve(src):
        dve("tensor_copy", dumd[0:1, 0:1], src[0:1, 0:1], inc="dve")

    def fl_act(src):
        act(duma[0:1, 0:1], src[0:1, 0:1], AF.Copy, inc="act")

    st = contextlib.ExitStack()
    y1T4 = sb_in(st, "y1T4", [128, KD * 4096], BF16)
    w1buf = sb_in(st, "w1buf", [128, 2048])
    w2buf = sb_in(st, "w2buf", [128, 2048])
    w1q = sb_in(st, "w1q", [128, 2048], BF16)
    w2q = sb_in(st, "w2q", [128, 2048], BF16)
    ltb = sb_in(st, "ltb", [128, 2048])
    ltb2 = sb_in(st, "ltb2", [128, 2048])
    sigb = sb_in(st, "sigb", [128, 512])
    hidb = [sb_in(st, f"hidb{i}", [128, 512], BF16) for i in range(2)]
    px1 = ps_in(st, "px1", [128, 512])
    px2 = ps_in(st, "px2", [128, 512])

    for kd in range(KD):
        for bb in range(B):
            off = kd * 4096 + bb * 2048
            dma("sync",
                y1T4[:, off:off + 2048]
                .rearrange("p (c j) -> p c j", c=NCH),
                sp["y1T"][bb, :, kd].transpose([1, 0, 2]), inc="y1tl")
    y1t_tok = P.tok("y1tl")

    w_tok = (None, 0)
    wq_free = (None, 0)
    hid_free = [(None, 0), (None, 0)]
    sig_free = (None, 0)
    px_free = [(None, 0), (None, 0)]
    for hb in range(HB):
        w8 = [wq_free] if wq_free[1] else []
        dma("sync", w1buf[:],
            io["w1t"][hb].rearrange("p kd c -> p (kd c)"),
            waits=w8, inc="w13")
        dma("sync", w2buf[:],
            io["w2t"][hb].rearrange("p kd c -> p (kd c)"), inc="w13")
        w_tok = P.tok("w13")
        qw = [w_tok, env["thr_tok"]]
        dve("tensor_scalar", ltb[:], w1buf[:], nthrc[:, 0:1], 0.0,
            ALU.is_lt, ALU.add, waits=qw, inc="dve")
        dve("scalar_tensor_tensor", w1q[:], w1buf[:], thrc[:, 0:1],
            ltb[:], ALU.is_gt, ALU.subtract, inc="dve")
        dve("tensor_scalar", ltb2[:], w2buf[:], nthrc[:, 1:2], 0.0,
            ALU.is_lt, ALU.add, inc="dve")
        dve("scalar_tensor_tensor", w2q[:], w2buf[:], thrc[:, 1:2],
            ltb2[:], ALU.is_gt, ALU.subtract, inc="dve")
        fl_dve(w2q)
        q_tok = P.tok("dve")
        wq_free = q_tok
        if "sp_w1qd" in debug:
            dma("gpsimd", sp["w1qd"][hb, 0], w1q[:], waits=[q_tok],
                inc="dg")
            dma("gpsimd", sp["w1qd"][hb, 1], w2q[:], waits=[q_tok],
                inc="dg")
            P.fence()
        for tb in range(8):
            hpr = tb % 2
            m1 = [q_tok, y1t_tok]
            if px_free[0][1]:
                m1.append(px_free[0])
            for kd in range(KD):
                mm(px1[:], w1q[:, kd * 128:(kd + 1) * 128],
                   y1T4[:, kd * 4096 + tb * 512:kd * 4096 + (tb + 1) * 512],
                   kd == 0, kd == KD - 1,
                   waits=m1 if kd == 0 else [],
                   inc="pe" if kd == KD - 1 else None)
            m2 = [px_free[1]] if px_free[1][1] else []
            for kd in range(KD):
                mm(px2[:], w2q[:, kd * 128:(kd + 1) * 128],
                   y1T4[:, kd * 4096 + tb * 512:kd * 4096 + (tb + 1) * 512],
                   kd == 0, kd == KD - 1,
                   waits=m2 if kd == 0 else [],
                   inc="pe" if kd == KD - 1 else None)
            mm2 = P.tok("pe")
            sw = [mm2]
            if sig_free[1]:
                sw.append(sig_free)
            act(sigb[:], px1[:], AF.Sigmoid, waits=sw, inc="act")
            fl_act(sigb)
            hw = [P.tok("act")]
            if hid_free[hpr][1]:
                hw.append(hid_free[hpr])
            dve("tensor_mul", hidb[hpr][:], sigb[:], px2[:],
                waits=hw, inc="dve")
            fl_dve(hidb[hpr])
            px_free[0] = P.tok("dve")
            px_free[1] = px_free[0]
            sig_free = px_free[0]
            sname = "hds" if hpr == 0 else "hdl"
            dma("gpsimd", sp["hid"][hb, tb], hidb[hpr][:],
                waits=[P.tok("dve")], inc=sname)
            hid_free[hpr] = P.tok(sname)
    st.close()
    P.barrier()

    st = contextlib.ExitStack()
    w3h = [sb_in(st, f"w3h{i}", [128, 22 * 512], BF16) for i in range(2)]
    hsl = sb_in(st, "hsl", [128, HB * 512], BF16)
    y2pre = sb_in(st, "y2pre", [128, 4 * 2048])
    y1ld = sb_in(st, "y1ld", [128, 4 * 2048])
    outb = sb_in(st, "outb", [128, 2048])
    cols = sb_in(st, "colsb", [128, 8])
    py2 = [ps_in(st, f"py2_{i}", [128, 512]) for i in range(4)]

    w3h_tok = [(None, 0), (None, 0)]
    w3h_free = [(None, 0), (None, 0)]
    h_tok = (None, 0)
    h_free = (None, 0)
    y1_tok = (None, 0)
    y1_free = (None, 0)
    py_free = [(None, 0)] * 4
    y2_free = (None, 0)
    out_free = (None, 0)

    for tb in range(8):
        b = tb // 4
        t0 = (tb % 4) * 512
        hw = [h_free] if h_free[1] else []
        dma("sync", hsl[:].rearrange("p (h c) -> p h c", h=HB),
            sp["hid"][:, tb].transpose([1, 0, 2]),
            waits=hw + [P.tok("hds"), P.tok("hdl")], inc="y1tl")
        h_tok = P.tok("y1tl")
        yw = [y1_free] if y1_free[1] else []
        for ts_ in range(4):
            cc = (t0 + ts_ * 128) // 128
            dma("scalar", y1ld[:, ts_ * 2048:(ts_ + 1) * 2048],
                sp["y1"][b, cc], waits=yw if ts_ == 0 else [], inc="y1l")
        y1_tok = P.tok("y1l")
        for db in range(4):
            for half in range(2):
                par = half
                sname = "w3l" if par == 0 else "w3ql"
                w8 = [w3h_free[par]] if w3h_free[par][1] else []
                dma("sync",
                    w3h[par][:].rearrange("p (h c) -> p h c", h=22),
                    sp["w3q"][db, half * 22:half * 22 + 22]
                    .transpose([1, 0, 2]), waits=w8, inc=sname)
                w3h_tok[par] = P.tok(sname)
                for ts_ in range(4):
                    mw = [w3h_tok[par], h_tok]
                    if half == 0 and py_free[ts_][1]:
                        mw.append(py_free[ts_])
                    for kh in range(22):
                        mm(py2[ts_][:],
                           hsl[:, (half * 22 + kh) * 512 + ts_ * 128:
                               (half * 22 + kh) * 512 + ts_ * 128 + 128],
                           w3h[par][:, kh * 512:(kh + 1) * 512],
                           half == 0 and kh == 0,
                           half == 1 and kh == 21,
                           waits=mw if kh == 0 else [],
                           inc="pe" if kh == 21 else None)
                w3h_free[par] = P.tok("pe")
            e_mm = P.tok("pe")
            for ts_ in range(4):
                ew = [e_mm, y1_tok]
                if y2_free[1] and db == 0 and ts_ == 0:
                    ew.append(y2_free)
                dve("tensor_add",
                    y2pre[:, ts_ * 2048 + db * 512:
                          ts_ * 2048 + (db + 1) * 512],
                    py2[ts_][:],
                    y1ld[:, ts_ * 2048 + db * 512:
                         ts_ * 2048 + (db + 1) * 512],
                    waits=ew, inc="dve")
                py_free[ts_] = P.tok("dve")
        fl_dve(y2pre)
        pre_tok = P.tok("dve")
        y1_free = pre_tok
        h_free = P.tok("pe")
        for ts_ in range(4):
            row = slice(ts_ * 2048, (ts_ + 1) * 2048)
            dve("tensor_reduce", cols[:, 0:1], y2pre[:, row], AX.X,
                ALU.add, waits=[pre_tok], inc="dve")
            fl_dve(cols)
            dve("memset", cols[:, 1:2], 0.0, inc="dve")
            fl_dve(cols)
            ow = [P.tok("dve")]
            if out_free[1]:
                ow.append(out_free)
            act(outb[:], y2pre[:, row], AF.Square,
                accum_out=cols[:, 1:2], waits=ow, inc="act")
            fl_act(cols)
            dve("tensor_scalar", cols[:, 2:3], cols[:, 0:1], 1.0 / D,
                0.0, ALU.mult, ALU.add, waits=[P.tok("act")], inc="dve")
            dve("tensor_scalar", cols[:, 3:4], cols[:, 1:2], 1.0 / D,
                0.0, ALU.mult, ALU.add, inc="dve")
            dve("scalar_tensor_tensor", cols[:, 6:7], cols[:, 2:3],
                cols[:, 2:3], cols[:, 3:4], ALU.mult,
                ALU.subtract, waits=[P.tok("dve")], inc="dve")
            act(cols[:, 4:5], cols[:, 6:7], AF.Ln, scale=-1.0,
                bias=cpool[:, 3:4], waits=[P.tok("dve")], inc="act")
            act(cols[:, 4:5], cols[:, 4:5], AF.Exp, scale=-0.5,
                waits=[P.tok("act")], inc="act")
            fl_act(cols)
            dve("tensor_scalar", outb[:], y2pre[:, row], cols[:, 2:3],
                cols[:, 4:5], ALU.subtract, ALU.mult,
                waits=[P.tok("act")], inc="dve")
            fl_dve(outb)
            dma("gpsimd",
                io["out"][b, t0 + ts_ * 128:t0 + (ts_ + 1) * 128, :],
                outb[:], waits=[P.tok("dve")], inc="dx")
            out_free = P.tok("dx")
        y2_free = P.tok("dx")
    st.close()


def host_layout(inputs):
    x = np.ascontiguousarray(
        np.asarray(inputs["L0_attn_hidden_states"], np.float32))
    wqkv = np.asarray(inputs["L0_attn_W_qkv"], np.float32)
    wz = np.asarray(inputs["L0_attn_W_z"], np.float32)
    wa = np.asarray(inputs["L0_attn_W_a"], np.float32)
    wb = np.asarray(inputs["L0_attn_W_b"], np.float32)
    convw = np.asarray(inputs["L0_attn_conv_w"], np.float32)
    dtbv = np.asarray(inputs["L0_attn_dt_bias"], np.float32)
    alogv = np.asarray(inputs["L0_attn_A_log"], np.float32)
    normw = np.asarray(inputs["L0_attn_norm_w"], np.float32)
    wout = np.ascontiguousarray(np.asarray(inputs["L0_attn_W_out"],
                                           np.float32))
    w1 = np.asarray(inputs["L0_ffn_w1"], np.float32)
    w2 = np.asarray(inputs["L0_ffn_w2"], np.float32)
    w3 = np.asarray(inputs["L0_ffn_w3"], np.float32)

    com = {}
    com["wqkv"] = np.ascontiguousarray(
        wqkv.reshape(KD, 128, CB, 128).transpose(2, 1, 0, 3))
    com["wz"] = np.ascontiguousarray(
        wz.reshape(KD, 128, 8, 512).transpose(2, 0, 1, 3))
    com["wab"] = np.ascontiguousarray(
        np.concatenate([wa, wb], axis=1).reshape(KD, 128, 64))
    com["wout"] = wout
    com["convw"] = np.ascontiguousarray(
        convw.reshape(CB, 128, KC).transpose(1, 0, 2).reshape(128, CB * KC))
    com["alog"] = np.ascontiguousarray(alogv.reshape(NV, 1))
    com["dtb"] = np.ascontiguousarray(dtbv.reshape(NV, 1))
    com["normw4"] = np.ascontiguousarray(np.tile(normw, 4).reshape(1, 512))
    w1t = np.ascontiguousarray(w1.T)
    w2t = np.ascontiguousarray(w2.T)
    w3t = np.ascontiguousarray(w3.T)
    com["w1t"] = np.ascontiguousarray(
        w1t.reshape(KD, 128, HB, 128).transpose(2, 1, 0, 3))
    com["w2t"] = np.ascontiguousarray(
        w2t.reshape(KD, 128, HB, 128).transpose(2, 1, 0, 3))
    com["w3t"] = np.ascontiguousarray(
        w3t.reshape(HB, 128, 4, 512).transpose(2, 0, 1, 3))
    com["identf"] = np.eye(128, dtype=np.float32)
    com["identb"] = np.eye(128, dtype=np.float32).astype(BFNP)
    mAu = np.full((128, 128), -1e30, np.float32)
    mAu[np.triu(np.ones((128, 128), bool), 1)] = 0.0
    mM = np.full((128, 128), -1e30, np.float32)
    mM[np.triu(np.ones((128, 128), bool), 0)] = 0.0
    com["maskAu4"] = np.ascontiguousarray(np.tile(mAu, (1, 4)))
    com["maskM4"] = np.ascontiguousarray(np.tile(mM, (1, 4)))
    com["onesb"] = np.ones((128, 1), np.float32).astype(BFNP)
    com["onesf"] = np.ones((1, 128), np.float32)
    com["onescf"] = np.ones((128, 1), np.float32)
    cp = np.zeros((128, 8), np.float32)
    cp[:, 0] = 1e-6
    cp[:, 1] = 1.0
    cp[:, 2] = -HALFLN
    cp[:, 3] = 1e-5
    com["cpool"] = cp

    in_maps = []
    for core in range(NCORE):
        m = dict(com)
        xs = np.ascontiguousarray(x[2 * core:2 * core + 2])
        m["x"] = xs
        m["xT"] = np.ascontiguousarray(xs.transpose(0, 2, 1))
        in_maps.append(m)
    return in_maps


def kernel(**inputs):
    nc, io, sp = build_nc()
    in_maps = host_layout(inputs)
    res = run_bass_kernel_spmd(nc, in_maps, core_ids=list(range(NCORE)))
    out = np.concatenate([r["out"] for r in res.results], axis=0)
    return out.astype(np.float32)


# version 09
# speedup vs baseline: 1.0649x; 1.0649x over previous
import contextlib
import numpy as np
import ml_dtypes
import concourse.bass as bass
import concourse.mybir as mybir
from concourse.bass_utils import run_bass_kernel_spmd

F32 = mybir.dt.float32
BF16 = mybir.dt.bfloat16
AF = mybir.ActivationFunctionType
ALU = mybir.AluOpType
AX = mybir.AxisListType
BFNP = ml_dtypes.bfloat16

NCORE = 8
B = 2
T = 2048
D = 2048
CONVD = 8192
ZD = 4096
NV, NK, DK, DV = 32, 16, 128, 128
KC = 4
C = 128
NCH = T // C
HD = 5632
HB = HD // 128
KD = D // 128
CB = CONVD // 128
HALFLN = 0.5 * float(np.log(DK))
NW = float(HD * D)

ENGS = ("tensor", "vector", "scalar", "gpsimd", "sync")


class Prog:

    def __init__(self, nc):
        self.nc = nc
        self.q = {e: [] for e in ENGS}
        self.cnt = {}
        self.sem = {}

    def new_sem(self, stack, name):
        self.sem[name] = stack.enter_context(self.nc.semaphore(name))
        self.cnt[name] = 0
        return name

    def emit(self, eng, fn, waits=(), inc=None, inc_amt=1):
        if inc is not None:
            self.cnt[inc] += inc_amt
        self.q[eng].append((fn, tuple(waits), inc, inc_amt))

    def tok(self, sem):
        return (sem, self.cnt[sem])

    def fence(self, sems=("pe", "dve", "act", "gps", "dw", "dg", "ds")):
        toks = tuple(self.tok(s) for s in sems if s in self.cnt)
        for e in ENGS:
            self.q[e].append((lambda eng: None, toks, None, 1))

    def barrier(self, name="bar"):
        self.fence()
        if name not in self.sem:
            raise KeyError(name)
        target = self.cnt[name] + len(ENGS)
        for e in ENGS:
            self.cnt[name] += 1
            self.q[e].append((lambda eng: eng.drain(), (), name, 1))
        for e in ENGS:
            self.q[e].append((lambda eng: None, ((name, target),), None, 1))

    def run(self, block):
        prog = self

        def mk(engname):
            def body(eng):
                last = {}
                for fn, waits, inc, amt in prog.q[engname]:
                    for s, v in waits:
                        if v > 0 and last.get(s, -1) < v:
                            eng.wait_ge(prog.sem[s], v)
                            last[s] = v
                    ins = fn(eng)
                    if inc is not None:
                        assert ins is not None
                        ins.then_inc(prog.sem[inc], amt)
            return body

        block.tensor(mk("tensor"))
        block.vector(mk("vector"))
        block.scalar(mk("scalar"))
        block.gpsimd(mk("gpsimd"))
        block.sync(mk("sync"))


def mk_helpers(P):
    def mm(out, lhsT, rhs, start, stop, waits=(), inc=None):
        P.emit("tensor",
               lambda e, o=out, l=lhsT, r=rhs, s=start, t=stop:
               e.matmul(o, l, r, start=s, stop=t), waits, inc)

    def ptrans(out, in_, ident, waits=(), inc=None):
        P.emit("tensor",
               lambda e, o=out, i=in_, d=ident: e.transpose(o, i, d),
               waits, inc)

    def dve(opname, *args, waits=(), inc=None, **kw):
        P.emit("vector",
               lambda e, a=args, k=kw, n=opname: getattr(e, n)(*a, **k),
               waits, inc)

    def gps(opname, *args, waits=(), inc=None, **kw):
        P.emit("gpsimd",
               lambda e, a=args, k=kw, n=opname: getattr(e, n)(*a, **k),
               waits, inc)

    def act(out, in_, func, waits=(), inc=None, **kw):
        P.emit("scalar",
               lambda e, o=out, i=in_, f=func, k=kw:
               e.activation(o, i, f, **k), waits, inc)

    def dma(queue, out, in_, waits=(), inc=None, slow=False):
        P.emit(queue,
               lambda e, o=out, i=in_, s=slow:
               e.dma_start(out=o, in_=i, allow_slow_non_contiguous=s)
               if s else e.dma_start(out=o, in_=i),
               waits, inc, inc_amt=16)

    def dmat(out, in_, waits=(), inc=None):
        P.emit("sync",
               lambda e, o=out, i=in_: e.dma_start_transpose(o, i),
               waits, inc, inc_amt=16)


    return mm, ptrans, dve, gps, act, dma, dmat


def build_nc(debug=()):
    stack = contextlib.ExitStack()
    nc = bass.Bass("TRN2", target_bir_lowering=False)
    nc._exit_stack = stack

    def dram(name, shape, dtype=F32, kind="ExternalInput"):
        if kind == "Internal" and name in debug:
            kind = "ExternalOutput"
        return nc.dram_tensor(name, shape, dtype, kind=kind).ap()

    io = {}
    io["x"] = dram("x", [B, T, D])
    io["xT"] = dram("xT", [B, D, T])
    io["wqkv"] = dram("wqkv", [CB, 128, KD, 128])
    io["wz"] = dram("wz", [8, KD, 128, 512])
    io["wab"] = dram("wab", [KD, 128, 64])
    io["wout"] = dram("wout", [NV * DV, D])
    io["convw"] = dram("convw", [128, CB * KC])
    io["alog"] = dram("alog", [NV, 1])
    io["dtb"] = dram("dtb", [NV, 1])
    io["normw4"] = dram("normw4", [1, 512])
    io["w1t"] = dram("w1t", [HB, 128, KD, 128])
    io["w2t"] = dram("w2t", [HB, 128, KD, 128])
    io["w3t"] = dram("w3t", [4, HB, 128, 512])
    io["identf"] = dram("identf", [128, 128])
    io["identb"] = dram("identb", [128, 128], BF16)
    io["maskAu4"] = dram("maskAu4", [128, 512])
    io["maskM4"] = dram("maskM4", [128, 512])
    io["onesb"] = dram("onesb", [128, 1], BF16)
    io["onesf"] = dram("onesf", [1, 128])
    io["onescf"] = dram("onescf", [128, 1])
    io["cpool"] = dram("cpool", [128, 8])
    io["out"] = dram("out", [B, T, D], kind="ExternalOutput")

    sp = {}
    sp["qkvT"] = dram("sp_qkvT", [B, CB, 128, T], BF16, kind="Internal")
    sp["z"] = dram("sp_z", [B, NCH, 8, 128, 512], BF16, kind="Internal")
    sp["rows"] = dram("sp_rows", [B, 320, T], F32, kind="Internal")
    sp["exgc"] = dram("sp_exgc", [B, NV, NCH], F32, kind="Internal")
    sp["ogT"] = dram("sp_ogT", [B, NCH, 16, 128, 256], BF16, kind="Internal")
    sp["ogRAW"] = dram("sp_ogRAW", [B, NCH, 128, 4096], BF16,
                       kind="Internal")
    sp["o32d"] = dram("sp_o32d", [B, NCH, 8, 128, 512], F32,
                      kind="Internal")
    sp["ubfd"] = dram("sp_ubfd", [B, NCH, 8, 128, 512], BF16,
                      kind="Internal")
    sp["rcd"] = dram("sp_rcd", [B, NCH, 8, 128, 8], F32, kind="Internal")
    sp["y1"] = dram("sp_y1", [B, NCH, 128, D], F32, kind="Internal")
    sp["y1T"] = dram("sp_y1T", [B, NCH, KD, 128, 128], BF16, kind="Internal")
    sp["hid"] = dram("sp_hid", [HB, 8, 128, 512], BF16, kind="Internal")
    sp["w3q"] = dram("sp_w3q", [4, HB, 128, 512], BF16, kind="Internal")
    sp["tiny"] = dram("sp_tiny", [64, 64], F32, kind="Internal")
    sp["p3d"] = dram("sp_p3d", [4, 128, 2048], F32, kind="Internal")
    sp["w1qd"] = dram("sp_w1qd", [HB, 2, 128, 2048], BF16, kind="Internal")

    P = Prog(nc)
    for s in ["dw", "dg", "ds", "dx", "pe", "dve", "act", "gps", "bar",
              "fin", "scl", "xt", "wq", "wz", "wab", "spq0", "spq1",
              "spz0", "spz1", "lns0", "lns1", "abb", "rwb", "rwq", "rin",
              "exs", "exg", "qk0", "qk1", "v0", "v1", "z0", "z1", "s1a",
              "s1b", "s2l", "gb0", "gb1", "ogt", "ogs", "y1s", "y1ts",
              "hds", "w3s", "xr", "ogl", "w13", "w3l", "hdl", "y1l",
              "y1tl", "w3ql", "s2b"]:
        P.new_sem(stack, s)

    build_phases(nc, P, io, sp, stack, debug)

    blk = stack.enter_context(nc.Block())
    P.run(blk)
    stack.close()
    return nc, io, sp


def build_phases(nc, P, io, sp, ostack, debug):
    mm, ptrans, dve, gps, act, dma, dmat = mk_helpers(P)

    def sb_in(st, name, shape, dtype=F32):
        return st.enter_context(nc.sbuf_tensor("s_" + name, shape, dtype))

    def ps_in(st, name, shape, dtype=F32):
        return st.enter_context(nc.psum_tensor("p_" + name, shape, dtype))

    sbc = lambda n, s, d=F32: sb_in(ostack, n, s, d)
    identf = sbc("identf", [128, 128])
    identb = sbc("identb", [128, 128], BF16)
    maskAu4 = sbc("maskAu4", [128, 512])
    maskM4 = sbc("maskM4", [128, 512])
    dumd = sbc("dumd", [1, 8])
    duma = sbc("duma", [1, 8])
    onesb = sbc("onesb", [128, 1], BF16)
    onesf = sbc("onesf", [1, 128])
    onescf = sbc("onescf", [128, 1])
    convw = sbc("convw", [128, CB * KC])
    cpool = sbc("cpool", [128, 8])
    alog = sbc("alog", [NV, 1])
    dtb = sbc("dtb", [NV, 1])
    normwb = sbc("normwb", [128, 512])
    nexpalog = sbc("nexpalog", [NV, 1])
    gacc = sbc("gacc", [128, 4])

    for (t_, s_) in [(identf, io["identf"]), (identb, io["identb"]),
                     (onesb, io["onesb"]), (onesf, io["onesf"]),
                     (onescf, io["onescf"]), (convw, io["convw"]),
                     (cpool, io["cpool"]), (maskAu4, io["maskAu4"]),
                     (maskM4, io["maskM4"]),
                     (alog, io["alog"]), (dtb, io["dtb"])]:
        dma("sync", t_[:], s_[:], inc="dw")
    p0tok = P.tok("dw")
    nw4 = sbc("nw4", [1, 512])
    p0st = contextlib.ExitStack()
    pnw = ps_in(p0st, "pnw", [128, 512])
    dma("sync", nw4[:], io["normw4"][:], inc="dw")
    t0 = P.tok("dw")
    mm(pnw[:], onesf[:, :], nw4[:, :], True, True,
       waits=[p0tok, t0], inc="pe")
    dve("tensor_copy", normwb[:], pnw[:], waits=[P.tok("pe")], inc="dve")
    expalog = sbc("expalog", [NV, 1])
    act(expalog[:], alog[:], AF.Exp, waits=[p0tok], inc="act")
    dve("tensor_scalar", nexpalog[:], expalog[:], -1.0, 0.0, ALU.mult,
        ALU.add, waits=[P.tok("act")], inc="dve")

    mid = contextlib.ExitStack()
    ostack.enter_context(mid)
    exgcf = [sb_in(mid, f"exgcf{b}", [1, NV * NCH]) for b in range(B)]

    P.barrier()
    p0st.close()
    env = dict(locals())
    phase_p1(nc, P, io, sp, ostack, mid, debug, env)
    P.barrier()
    phase_p2(nc, P, io, sp, ostack, mid, debug, env)
    P.barrier()
    mid.close()
    phase_p3(nc, P, io, sp, ostack, debug, env)
    P.barrier()
    phase_p4(nc, P, io, sp, ostack, debug, env)

    fin_tok = P.tok("dx")
    for e in ENGS:
        P.q[e].append((lambda eng: None, (fin_tok,), None, 1))


def phase_p1(nc, P, io, sp, ostack, mid, debug, env):
    mm, ptrans, dve, gps, act, dma, dmat = mk_helpers(P)
    sb_in = env["sb_in"]
    ps_in = env["ps_in"]
    onesb, onesf, convw = env["onesb"], env["onesf"], env["convw"]
    identf = env["identf"]
    alog, dtb, nexpalog = env["alog"], env["dtb"], env["nexpalog"]
    exgcf = env["exgcf"]

    for b in range(B):
        bst = contextlib.ExitStack()
        xT = sb_in(bst, f"xT{b}", [128, KD * T], BF16)
        rowsland = sb_in(bst, f"rowsland{b}", [64, T])
        dma("gpsimd", xT[:].rearrange("p (kd t) -> p kd t", kd=KD),
            io["xT"][b].rearrange("(kd p) t -> p kd t", p=128),
            inc="xt")
        xT_tok = P.tok("xt")

        st = contextlib.ExitStack()
        wbuf = sb_in(st, f"wbuf{b}", [128, KD * 128])
        wb16 = [sb_in(st, f"wb16_{b}{i}", [128, KD * 128], BF16)
                for i in range(2)]
        mixrow = [sb_in(st, f"mixrow{b}{i}", [128, T + KC - 1])
                  for i in range(2)]
        convrow = [sb_in(st, f"convrow{b}{i}", [128, T]) for i in range(2)]
        qkvrow = [sb_in(st, f"qkvrow{b}{i}", [128, T], BF16)
                  for i in range(2)]
        sqrow = [sb_in(st, f"sqrow{b}{i}", [128, T], BF16) for i in range(2)]
        lnbuf = [sb_in(st, f"lnbuf{b}{i}", [1, T]) for i in range(2)]
        pq = [ps_in(st, f"pq{b}{i}", [128, 1024]) for i in range(2)]
        pss = ps_in(st, f"pss{b}", [1, 1024])

        for i in range(2):
            dve("memset", mixrow[i][:, 0:KC - 1], 0.0, inc="dve")

        conv_done = [(None, 0), (None, 0)]
        w_done = ("pe", 0)
        cast_done = [("pe", 0), ("pe", 0)]
        qkv_done = [(None, 0), (None, 0)]
        sq_done = [(None, 0), (None, 0)]
        ln_done = [(None, 0), (None, 0)]
        pq_done = [("dve", 0), ("dve", 0)]

        for cb in range(CB):
            pr = cb % 2
            wts = [w_done] if w_done[1] else []
            dma("sync", wbuf[:],
                io["wqkv"][cb].rearrange("p kd c -> p (kd c)"),
                waits=wts, inc="wq")
            cwts = [cast_done[pr]] if cast_done[pr][1] else []
            dve("tensor_copy", wb16[pr][:], wbuf[:],
                waits=cwts + [P.tok("wq")], inc="dve")
            cast_tok = P.tok("dve")
            mm_toks = []
            for tg in range(2):
                for kd in range(KD):
                    for tt in range(2):
                        tb = tg * 2 + tt
                        w8 = [cast_tok, xT_tok]
                        if kd == 0 and tt == 0 and pq_done[tg][1]:
                            w8.append(pq_done[tg])
                        mm(pq[tg][:, tt * 512:(tt + 1) * 512],
                           wb16[pr][:, kd * 128:(kd + 1) * 128],
                           xT[:, kd * T + tb * 512:kd * T + (tb + 1) * 512],
                           kd == 0, kd == KD - 1,
                           waits=w8, inc="pe" if kd == KD - 1 else None)
                mm_toks.append(P.tok("pe"))
            w_done = cast_tok
            cast_done[pr] = mm_toks[-1]
            for tg in range(2):
                dve("tensor_copy",
                    mixrow[pr][:, KC - 1 + tg * 1024:KC - 1 + (tg + 1) * 1024],
                    pq[tg][:], waits=[mm_toks[tg]], inc="dve")
                pq_done[tg] = P.tok("dve")
            evac_tok = P.tok("dve")
            cwt = [conv_done[pr]] if conv_done[pr][1] else []
            dve("tensor_scalar", convrow[pr][:],
                mixrow[pr][:, KC - 1:KC - 1 + T],
                convw[:, cb * KC + 3:cb * KC + 4], 0.0, ALU.mult, ALU.add,
                waits=cwt + [evac_tok], inc="dve")
            for i in (2, 1, 0):
                dve("scalar_tensor_tensor", convrow[pr][:],
                    mixrow[pr][:, i:i + T],
                    convw[:, cb * KC + i:cb * KC + i + 1], convrow[pr][:],
                    ALU.mult, ALU.add, inc="dve")
            dve("tensor_copy", env["dumd"][0:1, 0:1],
                convrow[pr][0:1, 0:1], inc="dve")
            conv_tok = P.tok("dve")
            qwt = [qkv_done[pr]] if qkv_done[pr][1] else []
            act(qkvrow[pr][:], convrow[pr][:], AF.Silu,
                waits=qwt + [conv_tok], inc="act")
            conv_done[pr] = P.tok("act")
            silu_tok = P.tok("act")
            dma("gpsimd", sp["qkvT"][b, cb], qkvrow[pr][:],
                waits=[silu_tok], inc="spq0" if pr == 0 else "spq1")
            qkv_done[pr] = P.tok("spq0" if pr == 0 else "spq1")
            if cb < 32:
                swt = [sq_done[pr]] if sq_done[pr][1] else []
                act(sqrow[pr][:], qkvrow[pr][:], AF.Square,
                    waits=swt + [silu_tok], inc="act")
                sq_tok = P.tok("act")
                lwt = [ln_done[pr]] if ln_done[pr][1] else []
                for tb in range(4):
                    mw = [sq_tok]
                    if tb >= 2:
                        mw.append(P.tok("act"))
                    mm(pss[:, (tb % 2) * 512:(tb % 2) * 512 + 512],
                       onesb[:, :],
                       sqrow[pr][:, tb * 512:(tb + 1) * 512],
                       True, True, waits=mw, inc="pe")
                    act(lnbuf[pr][:, tb * 512:(tb + 1) * 512],
                        pss[:, (tb % 2) * 512:(tb % 2) * 512 + 512],
                        AF.Ln, bias=env["cpool"][0:1, 0:1],
                        waits=lwt + [P.tok("pe")], inc="act")
                sq_done[pr] = P.tok("pe")
                ln_tok = P.tok("act")
                rowi = (240 + cb) if cb < 16 else (224 + cb - 16)
                dma("gpsimd", sp["rows"][b, rowi], lnbuf[pr][:],
                    waits=[ln_tok], inc="lns0" if pr == 0 else "lns1")
                ln_done[pr] = P.tok("lns0" if pr == 0 else "lns1")
        st.close()
        P.fence()

        st = contextlib.ExitStack()
        wzbuf = sb_in(st, f"wzbuf{b}", [128, KD * 512])
        wzb16 = [sb_in(st, f"wzb16_{b}{i}", [128, KD * 512], BF16)
                 for i in range(2)]
        wabb = sb_in(st, f"wabb{b}", [128, KD * 64], BF16)
        zrow = [sb_in(st, f"zrow{b}{i}", [128, 512], BF16) for i in range(2)]
        abf = sb_in(st, f"abf{b}", [128, 64])
        pz = [ps_in(st, f"pz{b}{i}", [128, 512]) for i in range(2)]
        pab = ps_in(st, f"pab{b}", [128, 64])
        pabt = ps_in(st, f"pabt{b}", [64, 128])

        dma("gpsimd", wabb[:].rearrange("p (kd a) -> p kd a", kd=KD),
            io["wab"].rearrange("kd p a -> p kd a"), inc="wab")
        wab_tok = P.tok("wab")
        zw_done = ("pe", 0)
        zcast_done = [("pe", 0), ("pe", 0)]
        zrow_done = [(None, 0), (None, 0)]
        pz_done = [(None, 0), (None, 0)]
        ab_evac = (None, 0)
        abt_evac = (None, 0)
        for zcb in range(8):
            zpr = zcb % 2
            zwt = [zw_done] if zw_done[1] else []
            dma("sync", wzbuf[:].rearrange("p (kd c) -> p kd c", kd=KD),
                io["wz"][zcb].rearrange("kd p c -> p kd c"),
                waits=zwt, inc="wz")
            zct = [zcast_done[zpr]] if zcast_done[zpr][1] else []
            dve("tensor_copy", wzb16[zpr][:], wzbuf[:],
                waits=zct + [P.tok("wz")], inc="dve")
            zcast = P.tok("dve")
            last_mm = ("pe", 0)
            for tb in range(16):
                tpr = tb % 2
                for kd in range(KD):
                    w8 = [zcast, xT_tok]
                    if kd == 0 and pz_done[tpr][1]:
                        w8.append(pz_done[tpr])
                    mm(pz[tpr][:],
                       xT[:, kd * T + tb * 128:kd * T + tb * 128 + 128],
                       wzb16[zpr][:, kd * 512:(kd + 1) * 512],
                       kd == 0, kd == KD - 1,
                       waits=w8, inc="pe" if kd == KD - 1 else None)
                zmm = P.tok("pe")
                last_mm = zmm
                if zcb == 0:
                    abw = [ab_evac] if ab_evac[1] else []
                    for kd in range(KD):
                        mm(pab[:],
                           xT[:, kd * T + tb * 128:kd * T + tb * 128 + 128],
                           wabb[:, kd * 64:(kd + 1) * 64],
                           kd == 0, kd == KD - 1,
                           waits=[wab_tok, xT_tok] + (abw if kd == 0 else []),
                           inc="pe" if kd == KD - 1 else None)
                    abmm = P.tok("pe")
                    aew = [abt_evac] if abt_evac[1] else []
                    dve("tensor_copy", abf[:], pab[:],
                        waits=[abmm] + aew, inc="dve")
                    ab_evac = P.tok("dve")
                    ptrans(pabt[:], abf[:], identf[:],
                           waits=[ab_evac], inc="pe")
                    abt_mm = P.tok("pe")
                    dve("tensor_copy", rowsland[:, tb * 128:(tb + 1) * 128],
                        pabt[:], waits=[abt_mm], inc="dve")
                    abt_evac = P.tok("dve")
                zewt = [zrow_done[tpr]] if zrow_done[tpr][1] else []
                dve("tensor_copy", zrow[tpr][:], pz[tpr][:],
                    waits=zewt + [zmm], inc="dve")
                pz_done[tpr] = P.tok("dve")
                dma("gpsimd", sp["z"][b, tb, zcb], zrow[tpr][:],
                    waits=[P.tok("dve")],
                    inc="spz0" if tpr == 0 else "spz1")
                zrow_done[tpr] = P.tok("spz0" if tpr == 0 else "spz1")
            zw_done = zcast
            zcast_done[zpr] = last_mm
        ab_all = P.tok("dve")
        dma("gpsimd", sp["rows"][b, 256:288, :], rowsland[0:32, :],
            waits=[ab_all], inc="abb")
        dma("gpsimd", sp["rows"][b, 288:320, :], rowsland[32:64, :],
            waits=[ab_all], inc="abb")
        st.close()
        P.fence()

        st = contextlib.ExitStack()
        rows_p1(nc, P, io, sp, st, b, env)
        st.close()
        bst.close()
        P.fence()

    for b in range(B):
        dma("sync", exgcf[b][0:1, :].rearrange("o (c h) -> o c h", h=NV),
            sp["exgc"][b].transpose([1, 0]).unsqueeze(0),
            waits=[P.tok("exs")], inc="exg", slow=True)
    env["flats_tok"] = P.tok("exg")


def rows_p1(nc, P, io, sp, st, b, env):
    mm, ptrans, dve, gps, act, dma, dmat = mk_helpers(P)
    sb_in, ps_in = env["sb_in"], env["ps_in"]
    dtb, nexpalog = env["dtb"], env["nexpalog"]
    dumd, duma = env["dumd"], env["duma"]

    def fl_dve_r(srct):
        dve("tensor_copy", dumd[0:1, 0:1], srct[0:1, 0:1], inc="dve")

    def fl_act_r(srct):
        act(duma[0:1, 0:1], srct[0:1, 0:1], AF.Copy, inc="act")

    apre = sb_in(st, f"apre{b}", [32, T])
    bpre = sb_in(st, f"bpre{b}", [32, T])
    tA = sb_in(st, f"tA{b}", [32, T])
    tB = sb_in(st, f"tB{b}", [32, T])
    grow = sb_in(st, f"grow{b}", [32, T])
    G = sb_in(st, f"G{b}", [32, T])
    lnb = sb_in(st, f"lnb{b}", [32, T])
    lkd = sb_in(st, f"lkd{b}", [32, T])
    lqd = sb_in(st, f"lqd{b}", [32, T])
    xq = sb_in(st, f"xq{b}", [32, T])
    cpp = [sb_in(st, f"cpp{i}_{b}", [32, 192]) for i in range(2)]
    gcg = sb_in(st, f"gcg{b}", [32, NCH])

    lw_tok = [P.tok("abb"), P.tok("lns0"), P.tok("lns1")]
    dma("sync", apre[:, :], sp["rows"][b, 256:288, :], waits=lw_tok,
        inc="rin")
    dma("sync", bpre[:, :], sp["rows"][b, 288:320, :], inc="rin")
    dma("sync", lkd[:, :],
        sp["rows"][b][224:240, :].unsqueeze(1).broadcast_to([16, 2, T]),
        inc="rin")
    dma("sync", lqd[:, :],
        sp["rows"][b][240:256, :].unsqueeze(1).broadcast_to([16, 2, T]),
        inc="rin")
    in_tok = P.tok("rin")

    act(tA[:, :], apre[:, :], AF.Abs, bias=dtb[0:32, :], waits=[in_tok],
        inc="act")
    act(tA[:, :], tA[:, :], AF.Exp, scale=-1.0, waits=[P.tok("act")],
        inc="act")
    act(tA[:, :], tA[:, :], AF.Ln, bias=1.0, waits=[P.tok("act")],
        inc="act")
    fl_act_r(tA)
    spa_tok = P.tok("act")
    dve("tensor_scalar", tB[:, :], apre[:, :], dtb[0:32, :], 0.0,
        ALU.add, ALU.max, waits=[in_tok], inc="dve")
    dve("tensor_add", tB[:, :], tB[:, :], tA[:, :], waits=[spa_tok],
        inc="dve")
    dve("tensor_scalar", grow[:, :], tB[:, :], nexpalog[0:32, :], 0.0,
        ALU.mult, ALU.add, inc="dve")
    fl_dve_r(grow)
    grow_tok = P.tok("dve")
    act(tA[:, :], bpre[:, :], AF.Abs, waits=[in_tok, grow_tok], inc="act")
    act(tA[:, :], tA[:, :], AF.Exp, scale=-1.0, waits=[P.tok("act")],
        inc="act")
    act(tA[:, :], tA[:, :], AF.Ln, bias=1.0, waits=[P.tok("act")],
        inc="act")
    fl_act_r(tA)
    spb_tok = P.tok("act")
    dve("tensor_scalar", tB[:, :], bpre[:, :], 0.0, None, ALU.min,
        waits=[spb_tok], inc="dve")
    dve("tensor_sub", lnb[:, :], tB[:, :], tA[:, :], inc="dve")
    lnb_tok = P.tok("dve")
    act(tB[:, :], bpre[:, :], AF.Sigmoid, waits=[lnb_tok], inc="act")
    fl_act_r(tB)
    dma("gpsimd", sp["rows"][b, 192:224, :], tB[:, :],
        waits=[P.tok("act")], inc="rwb")

    for i in range(2):
        dve("memset", cpp[i][:, 0:64], 0.0, inc="dve")
    pad_tok = P.tok("dve")
    for c in range(NCH):
        dve("tensor_copy", cpp[0][:, 64:192], grow[:, c * 128:(c + 1) * 128],
            waits=[grow_tok, pad_tok], inc="dve")
        srci = 0
        off = 1
        while off < 128:
            dve("tensor_add", cpp[1 - srci][:, 64:192], cpp[srci][:, 64:192],
                cpp[srci][:, 64 - off:192 - off], inc="dve")
            srci = 1 - srci
            off *= 2
        dve("tensor_copy", G[:, c * 128:(c + 1) * 128], cpp[srci][:, 64:192],
            inc="dve")
    fl_dve_r(G)
    G_tok = P.tok("dve")

    act(gcg[:, :], G[:, 127::128], AF.Exp, waits=[G_tok], inc="act")
    fl_act_r(gcg)
    dma("gpsimd", sp["exgc"][b], gcg[:, :], waits=[P.tok("act")],
        inc="exs")

    dve("tensor_scalar", tA[:, :], lkd[:, :], -0.5, 0.0, ALU.mult, ALU.add,
        waits=[in_tok, G_tok], inc="dve")
    dve("tensor_add", xq[:, :], G[:, :], tA[:, :], inc="dve")
    dve("tensor_add", xq[:, :], xq[:, :], lnb[:, :], inc="dve")
    fl_dve_r(xq)
    x1_tok = P.tok("dve")
    dma("gpsimd", sp["rows"][b, 0:32, :], xq[:, :], waits=[x1_tok],
        inc="rwq")
    act(tB[:, :], xq[:, :], AF.Exp, waits=[x1_tok], inc="act")
    fl_act_r(tB)
    dve("tensor_scalar", tB[:, :], tB[:, :], -1.0, 0.0, ALU.mult, ALU.add,
        waits=[P.tok("act")], inc="dve")
    fl_dve_r(tB)
    dma("gpsimd", sp["rows"][b, 96:128, :], tB[:, :], waits=[P.tok("dve")],
        inc="rwb")
    s0_dma = (P.tok("rwb"), P.tok("rwq"))
    dve("tensor_scalar", tA[:, :], lkd[:, :], 0.5, 0.0, ALU.mult, ALU.add,
        waits=list(s0_dma), inc="dve")
    dve("tensor_add", xq[:, :], G[:, :], tA[:, :], inc="dve")
    dve("tensor_scalar", xq[:, :], xq[:, :], -1.0, 0.0, ALU.mult, ALU.add,
        inc="dve")
    fl_dve_r(xq)
    x2_tok = P.tok("dve")
    dma("gpsimd", sp["rows"][b, 32:64, :], xq[:, :], waits=[x2_tok],
        inc="rwq")
    x2_dma = P.tok("rwq")
    dve("tensor_scalar", tA[:, :], lqd[:, :], -0.5, -HALFLN, ALU.mult,
        ALU.add, waits=[x2_dma], inc="dve")
    dve("tensor_add", xq[:, :], G[:, :], tA[:, :], inc="dve")
    fl_dve_r(xq)
    x3_tok = P.tok("dve")
    dma("gpsimd", sp["rows"][b, 64:96, :], xq[:, :], waits=[x3_tok],
        inc="rwq")
    act(tB[:, :], xq[:, :], AF.Exp, waits=[x3_tok], inc="act")
    fl_act_r(tB)
    dma("gpsimd", sp["rows"][b, 128:160, :], tB[:, :], waits=[P.tok("act")],
        inc="rwb")
    s1_dma = P.tok("rwb")
    for c in range(NCH):
        sl = slice(c * 128, (c + 1) * 128)
        dve("tensor_scalar", tA[:, sl], G[:, sl],
            G[:, c * 128 + 127:c * 128 + 128], -1.0, ALU.subtract,
            ALU.mult, waits=[s1_dma] if c == 0 else [], inc="dve")
    dve("scalar_tensor_tensor", tA[:, :], lkd[:, :], -0.5, tA[:, :],
        ALU.mult, ALU.add, inc="dve")
    fl_dve_r(tA)
    act(tB[:, :], tA[:, :], AF.Exp, waits=[P.tok("dve")], inc="act")
    fl_act_r(tB)
    dma("gpsimd", sp["rows"][b, 160:192, :], tB[:, :], waits=[P.tok("act")],
        inc="rwb")


def phase_p2(nc, P, io, sp, ostack, mid, debug, env):
    mm, ptrans, dve, gps, act, dma, dmat = mk_helpers(P)
    sb_in, ps_in = env["sb_in"], env["ps_in"]
    onesf, identf, identb = env["onesf"], env["identf"], env["identb"]
    maskAu4, maskM4 = env["maskAu4"], env["maskM4"]
    normwb, cpool = env["normwb"], env["cpool"]
    exgcf = env["exgcf"]
    flats_tok = env["flats_tok"]
    dumd, duma = env["dumd"], env["duma"]

    st = contextlib.ExitStack()
    S32 = sb_in(st, "S32", [128, 64 * 128])
    Sbf = sb_in(st, "Sbf", [128, 64 * 128], BF16)
    qsl = [sb_in(st, f"qsl{i}", [128, 16 * 128], BF16) for i in range(2)]
    ksl = [sb_in(st, f"ksl{i}", [128, 16 * 128], BF16) for i in range(2)]
    vsl = [sb_in(st, f"vsl{i}", [128, 32 * 128], BF16) for i in range(2)]
    zsl = [sb_in(st, f"zsl{i}", [128, 4096], BF16) for i in range(2)]
    stg1 = [sb_in(st, f"stg1_{i}", [1, 1024]) for i in range(2)]
    stg2 = sb_in(st, "stg2", [128, 128])
    stg2b = sb_in(st, "stg2b", [32, 128])
    scxg = [sb_in(st, f"scxg{i}", [128, 192]) for i in range(2)]
    etmpD = sb_in(st, "etmpD", [128, 512])
    etmpM = sb_in(st, "etmpM", [128, 512])
    rbD = [sb_in(st, f"rbD{i}", [128, 512]) for i in range(2)]
    rbM = [sb_in(st, f"rbM{i}", [128, 512]) for i in range(2)]
    dsD = sb_in(st, "dsD", [128, 512])
    dsM = sb_in(st, "dsM", [128, 512])
    ktm = [sb_in(st, f"ktm{i}", [128, 256], BF16) for i in range(2)]
    atneg = [sb_in(st, f"atneg{i}", [128, 512], BF16) for i in range(2)]
    aneg = [sb_in(st, f"aneg{i}", [128, 512], BF16) for i in range(2)]
    mhat = [sb_in(st, f"mhat{i}", [128, 512], BF16) for i in range(2)]
    bvf = [sb_in(st, f"bvf{i}", [128, 512]) for i in range(2)]
    apow = [sb_in(st, f"apow{i}", [128, 512], BF16) for i in range(2)]
    atpow = [sb_in(st, f"atpow{i}", [128, 512], BF16) for i in range(2)]
    xbf = sb_in(st, "xbf", [128, 512], BF16)
    ubf = sb_in(st, "ubf", [128, 512], BF16)
    kdq = sb_in(st, "kdq", [128, 512], BF16)
    t1g = sb_in(st, "t1g", [128, 512])
    og32 = [sb_in(st, f"og32_{i}", [128, 4096], BF16) for i in range(2)]
    szg = sb_in(st, "szg", [128, 4096], BF16)
    sz2 = sb_in(st, "sz2", [128, 512], BF16)
    sqd = sb_in(st, "sqd", [128, 128])
    rsum = sb_in(st, "rsum", [128, 32])
    rstd = sb_in(st, "rstd", [128, 32])
    og = sb_in(st, "og", [128, 4096], BF16)
    ogT = sb_in(st, "ogT", [128, 4096], BF16)

    pA = ps_in(st, "pA", [128, 512])
    pB = ps_in(st, "pB", [128, 512])
    pX = ps_in(st, "pX", [128, 512])
    pP = ps_in(st, "pP", [128, 512])
    pM = ps_in(st, "pM", [128, 512])
    pD = ps_in(st, "pD", [128, 512])
    pK = ps_in(st, "pK", [128, 512])
    pT = ps_in(st, "pT", [128, 512], BF16)

    def fl_dve(src):
        dve("tensor_copy", dumd[0:1, 0:1], src[0:1, 0:1], inc="dve")

    def fl_act(src):
        act(duma[0:1, 0:1], src[0:1, 0:1], AF.Copy, inc="act")

    dumg = sb_in(st, "dumg", [1, 8])

    def fl_gps(src):
        gps("tensor_copy", dumg[0:1, 0:1], src[0:1, 0:1], inc="gps")

    dve("memset", S32[:], 0.0, inc="dve")
    dve("memset", Sbf[:], 0.0, inc="dve")
    fl_dve(Sbf)
    init_tok = P.tok("dve")

    gacc = env["gacc"]
    gtmp = sb_in(st, "gtmp", [128, 1])
    gbuf = [sb_in(st, f"gbuf{i}", [128, 512]) for i in range(2)]
    dve("memset", gacc[:], 0.0, inc="dve")
    gamma_jobs = []
    for wi, wname in enumerate(("w1t", "w2t")):
        for hb in range(HB):
            for half in range(4):
                gamma_jobs.append(
                    (wi, io[wname][hb].rearrange("p kd c -> p (kd c)")
                     [:, half * 512:(half + 1) * 512], 512))
    for db in range(4):
        for hb in range(HB):
            gamma_jobs.append((2, io["w3t"][db, hb], 512))
    gj = {"i": 0, "buf": [(None, 0), (None, 0)], "red": (None, 0)}

    def gamma_step(njobs=18):
        if "no_gamma" in debug:
            return
        for _ in range(njobs):
            i = gj["i"]
            if i >= len(gamma_jobs):
                return
            gj["i"] = i + 1
            wi, apw, ncols = gamma_jobs[i]
            pr = i % 2
            w8 = [gj["buf"][pr]] if gj["buf"][pr][1] else []
            gbs = "gb0" if pr == 0 else "gb1"
            dma("scalar", gbuf[pr][:, 0:ncols], apw, waits=w8, inc=gbs)
            dve("tensor_reduce", gtmp[:], gbuf[pr][:, 0:ncols],
                AX.X, ALU.add, True, waits=[P.tok(gbs)], inc="dve")
            gj["buf"][pr] = P.tok("dve")
            dve("tensor_add", gacc[:, wi:wi + 1], gacc[:, wi:wi + 1],
                gtmp[:], waits=[P.tok("dve")], inc="dve")
            gj["red"] = P.tok("dve")

    env["gamma_step"] = gamma_step

    NT = (None, 0)

    def w(*toks):
        return [t for t in toks if t is not None and t[1]]

    tk = {
        "qk_free": [NT, NT], "v_free": [NT, NT], "z_free": [NT, NT],
        "qk_tok": [NT, NT], "v_tok": [NT, NT], "z_tok": [NT, NT],
        "stg1_free": [NT, NT], "stg1_tok": [NT, NT],
        "rbD_free": [NT, NT], "rbM_free": [NT, NT],
        "dsD_free": NT, "dsM_free": NT,
        "stg2_free": NT, "stg2b_free": NT,
        "scxg_free": [NT, NT], "scxg_tok": [NT, NT],
        "pD_free": NT, "pM_free": NT, "pK_free": NT, "pT_free": NT,
        "pP_free": NT, "pX_free": NT, "pA_free": NT, "pB_free": NT,
        "etD_free": NT, "etM_free": NT, "t1g_free": NT,
        "atneg_free": [NT, NT], "aneg_free": [NT, NT],
        "mhat_free": [NT, NT], "bvf_free": [NT, NT], "ktm_free": [NT, NT],
        "og32_free": [NT, NT], "og_free": NT, "ogT_free": NT,
        "rsum_free": NT, "rstd_free": NT, "szg_free": NT,
        "epi_o": NT, "silu_t": NT, "lnexp_t": NT, "og_done": NT,
        "sq_all": NT,
    }
    flq = []

    def fill(n=1):
        for _ in range(n):
            if flq:
                flq.pop(0)()
    state_tok = [[init_tok] * 8 for _ in range(B)]
    bt = {}

    def load_qkv_slabs(m):
        if m >= 2 * NCH:
            return
        b, c = m % 2, m // 2
        qs = "qk0" if b == 0 else "qk1"
        dma("scalar", qsl[b][:].rearrange("p (h t) -> p h t", h=16),
            sp["qkvT"][b, 0:16, :, c * 128:(c + 1) * 128]
            .transpose([1, 0, 2]), waits=w(tk["qk_free"][b]), inc=qs)
        dma("scalar", ksl[b][:].rearrange("p (h t) -> p h t", h=16),
            sp["qkvT"][b, 16:32, :, c * 128:(c + 1) * 128]
            .transpose([1, 0, 2]), waits=w(tk["qk_free"][b]), inc=qs)
        tk["qk_tok"][b] = P.tok(qs)
        vs = "v0" if b == 0 else "v1"
        dma("gpsimd", vsl[b][:].rearrange("p (h t) -> p h t", h=32),
            sp["qkvT"][b, 32:64, :, c * 128:(c + 1) * 128]
            .transpose([1, 0, 2]), waits=w(tk["v_free"][b]), inc=vs)
        tk["v_tok"][b] = P.tok(vs)

    def load_z_slab(m):
        if m >= 2 * NCH:
            return
        b, c = m % 2, m // 2
        zs = "z0" if b == 0 else "z1"
        dma("sync", zsl[b][:].rearrange("p (zc j) -> p zc j", zc=8),
            sp["z"][b, c].transpose([1, 0, 2]),
            waits=w(tk["z_free"][b]), inc=zs)
        tk["z_tok"][b] = P.tok(zs)

    def prep(m):
        if m >= 2 * NCH:
            return
        b, c, sx = m % 2, m // 2, m % 2
        csl = slice(c * 128, (c + 1) * 128)
        dma("sync", stg2[:], sp["rows"][b, 96:224, csl],
            waits=w(tk["stg2_free"]), inc="s2l")
        dma("sync", stg2b[:], sp["rows"][b, 32:64, csl],
            waits=w(tk["stg2b_free"]), inc="s2b")
        ptrans(pD[:, 0:128], stg2[:], identf[:],
               waits=[P.tok("s2l")] + w(tk["pD_free"]), inc="pe")
        ptrans(pD[:, 128:160], stg2b[:], identf[0:32, 0:32],
               waits=[P.tok("s2b")], inc="pe")
        mm(pD[:, 160:192], onesf[:, :],
           exgcf[b][0:1, c * 32:(c + 1) * 32], True, True,
           waits=[flats_tok], inc="pe")
        pe_t = P.tok("pe")
        dve("tensor_copy", scxg[sx][:], pD[:, 0:192],
            waits=[pe_t] + w(tk["scxg_free"][sx]), inc="dve")
        fl_dve(scxg[sx])
        tok = P.tok("dve")
        tk["scxg_tok"][sx] = tok
        tk["stg2_free"] = tok
        tk["stg2b_free"] = tok
        tk["pD_free"] = tok

    PART_ORDER = ("masks", "kk", "expd", "expm", "vtrans_bvf",
                  "mhat", "atneg_aneg", "ktm")

    def build_parts(m, g):
        b, c, gp = m % 2, m // 2, g % 2
        sx = m % 2
        u0 = 4 * g
        sc = scxg[sx]
        csl = slice(c * 128, (c + 1) * 128)
        s1s = "s1a" if gp == 0 else "s1b"
        ent = bt.setdefault(g, {})

        def p_masks():
            dma("sync", rbD[gp][:].rearrange("p (u j) -> p u j", j=128),
                sp["rows"][b][u0:u0 + 4, csl].unsqueeze(0)
                .broadcast_to([128, 4, 128]),
                waits=w(tk["rbD_free"][gp]), inc=s1s)
            dma("sync", rbM[gp][:].rearrange("p (u j) -> p u j", j=128),
                sp["rows"][b][64 + u0:64 + u0 + 4, csl].unsqueeze(0)
                .broadcast_to([128, 4, 128]),
                waits=w(tk["rbM_free"][gp]), inc=s1s)
            ent["rb"] = P.tok(s1s)

        def p_kk():
            for pi in range(2):
                hs = 2 * g + pi
                kT = ksl[b][:, hs * 128:hs * 128 + 128]
                qT = qsl[b][:, hs * 128:hs * 128 + 128]
                mm(pK[:, pi * 256:pi * 256 + 128], kT, kT, True, True,
                   waits=([tk["qk_tok"][b]] + w(tk["pK_free"]))
                   if pi == 0 else [], inc=None)
                mm(pK[:, pi * 256 + 128:pi * 256 + 256], kT, qT, True,
                   True, inc="pe" if pi == 1 else None)
            ent["kk"] = P.tok("pe")

        def p_expd():
            gps("tensor_add", dsD[:], rbD[gp][:], maskAu4[:],
                waits=[ent["rb"]] + w(tk["dsD_free"]), inc="gps")
            fl_gps(dsD)
            da_t = P.tok("gps")
            tk["rbD_free"][gp] = da_t
            for du in range(4):
                u = u0 + du
                sl = slice(du * 128, (du + 1) * 128)
                act(etmpD[:, sl], dsD[:, sl], AF.Exp,
                    bias=sc[:, 128 + u:129 + u],
                    waits=([da_t, tk["scxg_tok"][sx]]
                           + w(tk["etD_free"])) if du == 0 else [],
                    inc="act")
            fl_act(etmpD)
            ent["expd"] = P.tok("act")
            tk["dsD_free"] = ent["expd"]

        def p_expm():
            gps("tensor_add", dsM[:], rbM[gp][:], maskM4[:],
                waits=[ent["rb"]] + w(tk["dsM_free"]), inc="gps")
            fl_gps(dsM)
            ma_t = P.tok("gps")
            tk["rbM_free"][gp] = ma_t
            for du in range(4):
                u = u0 + du
                sl = slice(du * 128, (du + 1) * 128)
                act(etmpM[:, sl], dsM[:, sl], AF.Exp,
                    bias=sc[:, 128 + u:129 + u],
                    waits=([ma_t, tk["scxg_tok"][sx]]
                           + w(tk["etM_free"])) if du == 0 else [],
                    inc="act")
            fl_act(etmpM)
            ent["expm"] = P.tok("act")
            tk["dsM_free"] = ent["expm"]

        def p_vtrans_bvf():
            for du in range(4):
                u = u0 + du
                ptrans(pT[:, du * 128:(du + 1) * 128],
                       vsl[b][:, u * 128:(u + 1) * 128], identb[:],
                       waits=([tk["v_tok"][b]] + w(tk["pT_free"]))
                       if du == 0 else [], inc="pe" if du == 3 else None)
            vt_mm = P.tok("pe")
            for du in range(4):
                u = u0 + du
                sl = slice(du * 128, (du + 1) * 128)
                act(bvf[gp][:, sl], pT[:, sl], AF.Copy,
                    scale=sc[:, 96 + u:97 + u],
                    waits=([vt_mm, tk["scxg_tok"][sx]]
                           + w(tk["bvf_free"][gp]))
                    if du == 0 else [], inc="act")
            fl_act(bvf[gp])
            ent["bv"] = P.tok("act")

        def p_atneg_aneg():
            for du in range(4):
                hpi = du // 2
                sl = slice(du * 128, (du + 1) * 128)
                dve("scalar_tensor_tensor", atneg[gp][:, sl],
                    pK[:, hpi * 256:hpi * 256 + 128], -1.0, etmpD[:, sl],
                    ALU.mult, ALU.mult,
                    waits=([ent["kk"], ent["expd"]]
                           + w(tk["atneg_free"][gp]))
                    if du == 0 else [], inc="dve")
            fl_dve(atneg[gp])
            ent["at"] = P.tok("dve")
            tk["etD_free"] = ent["at"]
            tk["pK_free"] = ent["at"]
            for du in range(4):
                sl = slice(du * 128, (du + 1) * 128)
                ptrans(pT[:, sl], atneg[gp][:, sl], identb[:],
                       waits=[ent["at"], ent["bv"]] if du == 0 else [],
                       inc="pe" if du == 3 else None)
            an_mm = P.tok("pe")
            dve("tensor_copy", aneg[gp][:], pT[:],
                waits=[an_mm] + w(tk["aneg_free"][gp]), inc="dve")
            fl_dve(aneg[gp])
            ent["an"] = P.tok("dve")

        def p_mhat():
            for du in range(4):
                hpi = du // 2
                sl = slice(du * 128, (du + 1) * 128)
                dve("tensor_mul", mhat[gp][:, sl], etmpM[:, sl],
                    pK[:, hpi * 256 + 128:hpi * 256 + 256],
                    waits=([ent["expm"], ent["kk"]]
                           + w(tk["mhat_free"][gp]))
                    if du == 0 else [], inc="dve")
            fl_dve(mhat[gp])
            ent["mh"] = P.tok("dve")
            tk["etM_free"] = ent["mh"]

        def p_ktm():
            for pi in range(2):
                hs = 2 * g + pi
                ptrans(pT[:, pi * 128:(pi + 1) * 128],
                       ksl[b][:, hs * 128:hs * 128 + 128], identb[:],
                       waits=[ent["an"], tk["qk_tok"][b]]
                       if pi == 0 else [], inc="pe" if pi == 1 else None)
            ktm_mm = P.tok("pe")
            dve("tensor_copy", ktm[gp][:], pT[:, 0:256],
                waits=[ktm_mm] + w(tk["ktm_free"][gp]), inc="dve")
            fl_dve(ktm[gp])
            ent["ktm"] = P.tok("dve")
            tk["pT_free"] = ent["ktm"]

        return dict(masks=p_masks, kk=p_kk, expd=p_expd, expm=p_expm,
                    vtrans_bvf=p_vtrans_bvf, atneg_aneg=p_atneg_aneg,
                    mhat=p_mhat, ktm=p_ktm)

    def solve(m, g, parts=None):
        b, c, gp = m % 2, m // 2, g % 2
        sx = m % 2
        u0 = 4 * g
        sc = scxg[sx]
        bg = bt.pop(g)
        sw = state_tok[b][g]
        for pi in range(2):
            hs = 2 * g + pi
            kT = ksl[b][:, hs * 128:hs * 128 + 128]
            mw = ([sw, tk["qk_tok"][b]] + w(tk["pP_free"])) \
                if pi == 0 else []
            mm(pP[:, pi * 256:(pi + 1) * 256], kT,
               Sbf[:, (b * 32 + u0 + 2 * pi) * 128:
                   (b * 32 + u0 + 2 * pi + 2) * 128],
               True, True, waits=mw, inc="pe" if pi == 1 else None)
        p_mm = P.tok("pe")
        if parts:
            parts["masks"]()
        for du in range(4):
            u = u0 + du
            sl = slice(du * 128, (du + 1) * 128)
            dve("scalar_tensor_tensor", t1g[:, sl], pP[:, sl],
                sc[:, u:u + 1], bvf[gp][:, sl], ALU.mult, ALU.add,
                waits=([p_mm, bg["bv"], tk["scxg_tok"][sx]]
                       + w(tk["t1g_free"])) if du == 0 else [], inc="dve")
        fl_dve(t1g)
        x_tok = P.tok("dve")
        tk["bvf_free"][gp] = x_tok
        if parts:
            parts["vtrans_bvf"]()
        acur, atcur = aneg[gp], atneg[gp]
        pow_evac = (bg["an"], bg["at"])
        for j in range(7):
            if j == 0:
                dve("tensor_copy", xbf[:], t1g[:], waits=[x_tok],
                    inc="dve")
                fl_dve(xbf)
                snap = P.tok("dve")
            elif j % 2 == 1:
                act(xbf[:], pX[:], AF.Copy, waits=[x_tok], inc="act")
                fl_act(xbf)
                snap = P.tok("act")
            else:
                dve("tensor_copy", xbf[:], pX[:], waits=[x_tok], inc="dve")
                fl_dve(xbf)
                snap = P.tok("dve")
            if j < 6:
                half = j == 5
                sqw = [pow_evac[0], pow_evac[1]]
                if j == 0:
                    sqw += w(tk["pA_free"], tk["pB_free"])
                for du in range(4):
                    sl = slice(du * 128, (du + 1) * 128)
                    if not half:
                        mm(pA[:, sl], atcur[:, sl], acur[:, sl], True,
                           True, waits=sqw if du == 0 else [], inc=None)
                    mm(pB[:, sl], acur[:, sl], atcur[:, sl], True, True,
                       waits=sqw if (du == 0 and half) else [],
                       inc="pe" if du == 3 else None)
                sq_mm = P.tok("pe")
                if not half:
                    act(apow[j % 2][:], pA[:], AF.Copy, waits=[sq_mm],
                        inc="act")
                    fl_act(apow[j % 2])
                    ap_t = P.tok("act")
                    if j == 4:
                        tk["pA_free"] = ap_t
                dve("tensor_copy", atpow[j % 2][:], pB[:], waits=[sq_mm],
                    inc="dve")
                fl_dve(atpow[j % 2])
                atp_t = P.tok("dve")
                if half:
                    tk["pB_free"] = atp_t
                    next_evac = (atp_t, atp_t)
                else:
                    next_evac = (ap_t, atp_t)
            lw = [snap, pow_evac[0], pow_evac[1]]
            if j == 0:
                lw += w(tk["pX_free"])
            for du in range(4):
                sl = slice(du * 128, (du + 1) * 128)
                if j == 0:
                    mm(pX[:, sl], atcur[:, sl], xbf[:, sl], du == 0, False,
                       waits=lw if du == 0 else [], inc=None)
                    mm(pX[:, sl], identb[:], xbf[:, sl], False, True,
                       inc="pe" if du == 3 else None)
                else:
                    mm(pX[:, sl], atcur[:, sl], xbf[:, sl], False, True,
                       waits=lw if du == 0 else [],
                       inc="pe" if du == 3 else None)
            x_tok = P.tok("pe")
            if parts:
                if j == 1:
                    parts["kk"]()
                elif j == 3:
                    parts["expd"]()
                elif j == 5:
                    parts["expm"]()
            if j in (2, 6):
                fill()
            if j < 6:
                acur = None if half else apow[j % 2]
                atcur = atpow[j % 2]
                pow_evac = next_evac
        dve("tensor_copy", ubf[:], pX[:], waits=[x_tok], inc="dve")
        fl_dve(ubf)
        u_tok = P.tok("dve")
        tk["atneg_free"][gp] = x_tok
        tk["aneg_free"][gp] = x_tok
        for pi in range(2):
            hs = 2 * g + pi
            qT = qsl[b][:, hs * 128:hs * 128 + 128]
            mm(pP[:, pi * 256:(pi + 1) * 256], qT,
               Sbf[:, (b * 32 + u0 + 2 * pi) * 128:
                   (b * 32 + u0 + 2 * pi + 2) * 128],
               True, True, waits=[x_tok] if pi == 0 else [],
               inc="pe" if pi == 1 else None)
        qs_mm = P.tok("pe")
        for du in range(4):
            sl = slice(du * 128, (du + 1) * 128)
            mm(pX[:, sl], mhat[gp][:, sl], ubf[:, sl], True, True,
               waits=[u_tok, bg["mh"]] if du == 0 else [],
               inc="pe" if du == 3 else None)
        mu_mm = P.tok("pe")
        tk["mhat_free"][gp] = mu_mm
        if parts:
            parts["mhat"]()
        fill()
        for du in range(4):
            u = u0 + du
            sl = slice(du * 128, (du + 1) * 128)
            act(t1g[:, sl], pP[:, sl], AF.Copy,
                scale=sc[:, 32 + u:33 + u],
                waits=[qs_mm, u_tok] if du == 0 else [], inc="act")
        fl_act(t1g)
        co_t = P.tok("act")
        ph = m % 2
        ow = [co_t, mu_mm]
        if g == 0:
            ow += w(tk["og32_free"][ph])
        dve("tensor_add", og32[ph][:, g * 512:(g + 1) * 512], t1g[:],
            pX[:], waits=ow, inc="dve")
        fl_dve(og32[ph])
        o_t = P.tok("dve")
        tk["pX_free"] = o_t
        tk["t1g_free"] = o_t
        if parts:
            parts["atneg_aneg"]()
        for du in range(4):
            u = u0 + du
            hpi = du // 2
            sl = slice(du * 128, (du + 1) * 128)
            act(kdq[:, sl], ktm[gp][:, hpi * 128:hpi * 128 + 128],
                AF.Copy, scale=sc[:, 64 + u:65 + u],
                waits=[bg["ktm"], tk["scxg_tok"][sx]] if du == 0 else [],
                inc="act")
        fl_act(kdq)
        kd_t = P.tok("act")
        tk["ktm_free"][gp] = kd_t
        if parts:
            parts["ktm"]()
        for du in range(4):
            sl = slice(du * 128, (du + 1) * 128)
            mm(pP[:, sl], kdq[:, sl], ubf[:, sl], True, True,
               waits=[kd_t, u_tok, co_t] if du == 0 else [],
               inc="pe" if du == 3 else None)
        su_mm = P.tok("pe")
        for du in range(4):
            u = u0 + du
            usl = slice((b * 32 + u) * 128, (b * 32 + u + 1) * 128)
            sl = slice(du * 128, (du + 1) * 128)
            dve("scalar_tensor_tensor", S32[:, usl], S32[:, usl],
                sc[:, 160 + u:161 + u], pP[:, sl], ALU.mult, ALU.add,
                waits=[su_mm, tk["scxg_tok"][sx]] if du == 0 else [],
                inc="dve")
        gsl = slice((b * 32 + u0) * 128, (b * 32 + u0 + 4) * 128)
        dve("tensor_copy", Sbf[:, gsl], S32[:, gsl], inc="dve")
        fl_dve(Sbf)
        st_t = P.tok("dve")
        state_tok[b][g] = st_t
        tk["pP_free"] = st_t
        return o_t

    def epi_thunks(m):
        b, c, ph = m % 2, m // 2, m % 2
        ogf = og32[ph]

        def sq_piece(k):
            def f():
                fw = [tk["epi_o"]]
                if k == 0:
                    dve("memset", rsum[:], 0.0,
                        waits=w(tk["rsum_free"]), inc="dve")
                    fl_dve(rsum)
                    fw.append(P.tok("dve"))
                first = True
                for du in range(4):
                    u = k * 4 + du
                    sl = slice(u * 128, (u + 1) * 128)
                    act(sqd[:], ogf[:, sl], AF.Square,
                        accum_out=rsum[:, u:u + 1],
                        waits=fw if first else [], inc="act")
                    first = False
                if k == 7:
                    fl_act(rsum)
                    tk["sq_all"] = P.tok("act")
            return f

        def lnexp():
            act(rstd[:], rsum[:], AF.Ln, bias=cpool[:, 0:1],
                scale=1.0 / 128.0,
                waits=[tk["sq_all"]] + w(tk["rstd_free"]), inc="act")
            tk["rsum_free"] = P.tok("act")
            act(rstd[:], rstd[:], AF.Exp, scale=-0.5,
                waits=[P.tok("act")], inc="act")
            fl_act(rstd)
            tk["lnexp_t"] = P.tok("act")

        def silu_piece(k0):
            def f():
                zw = [tk["z_tok"][b]] + (w(tk["szg_free"])
                                         if k0 == 0 else [])
                for k in range(k0, k0 + 4):
                    act(szg[:, k * 512:(k + 1) * 512],
                        zsl[b][:, k * 512:(k + 1) * 512], AF.Silu,
                        waits=zw if k == k0 else [], inc="act")
                if k0 == 4:
                    fl_act(szg)
                    tk["z_free"][b] = P.tok("act")
                    tk["silu_t"] = P.tok("act")
            return f

        def mul_block(k0):
            def f():
                for k in (k0, k0 + 1):
                    mw = [tk["silu_t"], tk["lnexp_t"]]
                    if k == k0 == 0:
                        mw += w(tk["og_free"])
                    dve("tensor_mul", sz2[:],
                        szg[:, k * 512:(k + 1) * 512], normwb[:],
                        waits=mw if k == k0 else [], inc="dve")
                    for du in range(4):
                        u = k * 4 + du
                        sl = slice(u * 128, (u + 1) * 128)
                        dve("scalar_tensor_tensor", og[:, sl],
                            ogf[:, sl], rstd[:, u:u + 1],
                            sz2[:, du * 128:(du + 1) * 128],
                            ALU.mult, ALU.mult,
                            waits=[P.tok("dve")] if du == 0 else [],
                            inc="dve")
                if k0 == 6:
                    fl_dve(og)
                    tok = P.tok("dve")
                    tk["og_done"] = tok
                    tk["og32_free"][ph] = tok
                    tk["rstd_free"] = tok
                    tk["szg_free"] = tok
            return f

        def ogspill():
            otw = [tk["og_done"]] + w(tk["ogT_free"])
            dmat(ogT[:].rearrange("p (u j) -> p u j", u=32),
                 og[:].rearrange("p (u j) -> p u j", u=32),
                 waits=otw, inc="ogt")
            tk["og_free"] = P.tok("ogt")
            dma("gpsimd", sp["ogT"][b, c].rearrange("u p j -> p u j"),
                ogT[:].rearrange("p (u j) -> p u j", u=32),
                waits=[tk["og_free"]], inc="ogs")
            tk["ogT_free"] = P.tok("ogs")

        def mul_last():
            mul_block(6)()
            ogspill()

        return ([sq_piece(k) for k in range(8)]
                + [lnexp, silu_piece(0), silu_piece(4),
                   mul_block(0), mul_block(2), mul_block(4), mul_last])

    pending = []
    load_qkv_slabs(0)
    load_z_slab(0)
    prep(0)
    for g0 in (0, 1):
        bp = build_parts(0, g0)
        for k in PART_ORDER:
            bp[k]()
    for m in range(2 * NCH):
        b = m % 2
        flq.extend(pending)
        pending = []
        for _ in range(6):
            flq.append(lambda: gamma_step(3))
        load_qkv_slabs(m + 1)
        last_o = None
        for g in range(8):
            if g < 6:
                parts = build_parts(m, g + 2)
            elif m + 1 < 2 * NCH:
                parts = build_parts(m + 1, g - 6)
            else:
                parts = None
            last_o = solve(m, g, parts)
            if g == 4:
                load_z_slab(m + 1)
            if g == 5:
                prep(m + 1)
        tk["qk_free"][b] = P.tok("pe")
        tk["v_free"][b] = P.tok("pe")
        tk["epi_o"] = last_o
        pending = epi_thunks(m)
    while flq:
        flq.pop(0)()
    for f in pending:
        f()
    while gj["i"] < len(gamma_jobs):
        gamma_step(20)
    env["gamma_red"] = P.tok("dve")
    st.close()


def phase_p3(nc, P, io, sp, ostack, debug, env):
    mm, ptrans, dve, gps, act, dma, dmat = mk_helpers(P)
    sb_in, ps_in = env["sb_in"], env["ps_in"]
    cpool = env["cpool"]
    dumd, duma = env["dumd"], env["duma"]
    gacc = env["gacc"]
    onesf, onescf = env["onesf"], env["onescf"]

    def fl_dve(src):
        dve("tensor_copy", dumd[0:1, 0:1], src[0:1, 0:1], inc="dve")

    def fl_act(src):
        act(duma[0:1, 0:1], src[0:1, 0:1], AF.Copy, inc="act")

    thrc = sb_in(ostack, "thrc", [128, 4])
    nthrc = sb_in(ostack, "nthrc", [128, 4])
    env["thrc"] = thrc
    env["nthrc"] = nthrc
    st = contextlib.ExitStack()
    wout = sb_in(st, "wout", [128, 32 * 2048], BF16)
    ogl = [sb_in(st, f"ogl{i}", [128, 4096], BF16) for i in range(2)]
    xld = [sb_in(st, f"xld{i}", [128, 2048]) for i in range(2)]
    y1pre = sb_in(st, "y1pre", [128, 2048])
    y1sb = sb_in(st, "y1sb", [128, 2048])
    y1b = sb_in(st, "y1b", [128, 2048], BF16)
    y1T = sb_in(st, "y1T", [128, 2048], BF16)
    cols = sb_in(st, "cols", [128, 8])
    w3buf = [sb_in(st, f"w3buf{i}", [128, 1024]) for i in range(2)]
    w3qb = [sb_in(st, f"w3qb{i}", [128, 1024], BF16) for i in range(2)]
    colsd = sb_in(st, "colsd", [128, 8])
    p3 = [ps_in(st, f"p3_{i}", [128, 512]) for i in range(2)]
    pg = ps_in(st, "pg", [128, 8])

    mm(pg[0:3, 0:1], gacc[:, 0:3], onescf[:, :], True, True,
       waits=[env["gamma_red"]], inc="pe")
    dve("tensor_copy", cols[0:3, 0:1], pg[0:3, 0:1], waits=[P.tok("pe")],
        inc="dve")
    dve("tensor_scalar", cols[0:3, 1:2], cols[0:3, 0:1],
        0.5 / NW, 0.5e-8, ALU.mult, ALU.add, waits=[P.tok("dve")],
        inc="dve")
    fl_dve(cols)
    dma("gpsimd", sp["tiny"][0:3, 0:1], cols[0:3, 1:2],
        waits=[P.tok("dve")], inc="w3s", slow=True)
    dma("sync", cols[0:1, 4:7], sp["tiny"][0:3, 0:1].transpose([1, 0]),
        waits=[P.tok("w3s")], inc="w3l", slow=True)
    mm(pg[:, 4:7], onesf[:, :], cols[0:1, 4:7], True, True,
       waits=[P.tok("w3l")], inc="pe")
    dve("tensor_copy", thrc[:, 0:3], pg[:, 4:7], waits=[P.tok("pe")],
        inc="dve")
    dve("tensor_scalar", nthrc[:, 0:3], thrc[:, 0:3], -1.0, 0.0,
        ALU.mult, ALU.add, waits=[P.tok("dve")], inc="dve")
    fl_dve(nthrc)
    env["thr_tok"] = P.tok("dve")
    if "sp_p3d" in debug:
        dma("gpsimd", sp["p3d"][2][:, 0:4], thrc[:], waits=[env["thr_tok"]],
            inc="dg")
        dma("gpsimd", sp["p3d"][2][:, 4:8], gacc[:], waits=[env["thr_tok"]],
            inc="dg")
        dma("gpsimd", sp["p3d"][3][:, 0:8], cols[:], waits=[env["thr_tok"]],
            inc="dg")

    w3jobs = []
    for db in range(4):
        for hb in range(0, HB, 2):
            w3jobs.append((db, hb))
    w3j = {"i": 0, "buf": [(None, 0), (None, 0)]}

    def w3_step(njobs=6):
        for _ in range(njobs):
            i = w3j["i"]
            if i >= len(w3jobs):
                return
            w3j["i"] = i + 1
            db, hb = w3jobs[i]
            pr = i % 2
            w8 = [w3j["buf"][pr]] if w3j["buf"][pr][1] else []
            sname = "gb0" if pr == 0 else "gb1"
            dma("scalar", w3buf[pr][:].rearrange("p (h c) -> p h c", h=2),
                io["w3t"][db, hb:hb + 2].transpose([1, 0, 2]),
                waits=w8, inc=sname)
            dve("tensor_scalar", w3qb[pr][:], w3buf[pr][:],
                nthrc[:, 2:3], 0.0, ALU.is_lt, ALU.add,
                waits=[P.tok(sname), env["thr_tok"]], inc="dve")
            dve("scalar_tensor_tensor", w3qb[pr][:], w3buf[pr][:],
                thrc[:, 2:3], w3qb[pr][:], ALU.is_gt, ALU.subtract,
                inc="dve")
            fl_dve(w3qb[pr])
            sspill = "w3ql" if pr == 0 else "hds"
            dma("gpsimd",
                sp["w3q"][db, hb:hb + 2].transpose([1, 0, 2]),
                w3qb[pr][:].rearrange("p (h c) -> p h c", h=2),
                waits=[P.tok("dve")], inc=sspill)
            w3j["buf"][pr] = P.tok(sspill)

    for i in range(4):
        dma("gpsimd",
            wout[:, i * 16384:(i + 1) * 16384]
            .rearrange("p (kc d) -> p kc d", kc=8),
            io["wout"][i * 1024:(i + 1) * 1024]
            .rearrange("(kc p) d -> p kc d", p=128), inc="w13")
    wout_tok = P.tok("w13")

    ogl_tok = [(None, 0), (None, 0)]
    x_tok = [(None, 0), (None, 0)]
    ogl_free = [(None, 0), (None, 0)]
    x_free = [(None, 0), (None, 0)]
    y1pre_free = (None, 0)
    y1sb_free = (None, 0)
    y1b_free = (None, 0)
    y1T_free = (None, 0)
    p3_free = [(None, 0), (None, 0)]

    def load_iter(b, c, pr):
        sname = "qk0" if pr == 0 else "qk1"
        w8 = [ogl_free[pr]] if ogl_free[pr][1] else []
        dma("sync", ogl[pr][:].rearrange("p (u j) -> p u j", u=16),
            sp["ogT"][b, c].transpose([1, 0, 2]), waits=w8, inc=sname)
        ogl_tok[pr] = P.tok(sname)
        sname = "v0" if pr == 0 else "v1"
        w8 = [x_free[pr]] if x_free[pr][1] else []
        dma("sync", xld[pr][:], io["x"][b, c * 128:(c + 1) * 128, :],
            waits=w8, inc=sname)
        x_tok[pr] = P.tok(sname)

    it = 0
    load_iter(0, 0, 0)
    for b in range(B):
        for c in range(NCH):
            pr = it % 2
            nb, nch = (b, c + 1) if c + 1 < NCH else (b + 1, 0)
            if nb < B:
                load_iter(nb, nch, 1 - pr)
            w3_step()
            for db in range(4):
                dpr = db % 2
                mw = [ogl_tok[pr], wout_tok]
                if p3_free[dpr][1]:
                    mw.append(p3_free[dpr])
                for kc in range(32):
                    mm(p3[dpr][:],
                       ogl[pr][:, kc * 128:(kc + 1) * 128],
                       wout[:, kc * 2048 + db * 512:
                            kc * 2048 + (db + 1) * 512],
                       kc == 0, kc == 31,
                       waits=mw if kc == 0 else [],
                       inc="pe" if kc == 31 else None)
                e_mm = P.tok("pe")
                ew = [e_mm, x_tok[pr]]
                if y1pre_free[1] and db == 0:
                    ew.append(y1pre_free)
                dve("tensor_add", y1pre[:, db * 512:(db + 1) * 512],
                    p3[dpr][:], xld[pr][:, db * 512:(db + 1) * 512],
                    waits=ew, inc="dve")
                p3_free[dpr] = P.tok("dve")
            fl_dve(y1pre)
            pre_tok = P.tok("dve")
            if "sp_p3d" in debug and b == 0 and c == 0:
                dma("gpsimd", sp["p3d"][0], y1pre[:], waits=[pre_tok],
                    inc="dg")
                dma("gpsimd", sp["p3d"][1], xld[pr][:],
                    waits=[x_tok[pr]], inc="dg")
                dma("gpsimd", sp["p3d"][2],
                    wout[:, 0:2048], waits=[wout_tok], inc="dg")
                dma("gpsimd", sp["p3d"][3], ogl[pr][:, 0:2048],
                    waits=[ogl_tok[pr]], inc="dg")
                P.fence()
            dve("tensor_reduce", cols[:, 0:1], y1pre[:], AX.X, ALU.add,
                waits=[pre_tok], inc="dve")
            fl_dve(cols)
            dve("memset", cols[:, 1:2], 0.0, inc="dve")
            fl_dve(cols)
            sw = [P.tok("dve"), pre_tok]
            if y1sb_free[1]:
                sw.append(y1sb_free)
            act(y1sb[:], y1pre[:], AF.Square, accum_out=cols[:, 1:2],
                waits=sw, inc="act")
            fl_act(cols)
            sq_tok = P.tok("act")
            dve("tensor_scalar", cols[:, 2:3], cols[:, 0:1], 1.0 / D, 0.0,
                ALU.mult, ALU.add, waits=[sq_tok], inc="dve")
            dve("tensor_scalar", cols[:, 3:4], cols[:, 1:2], 1.0 / D, 0.0,
                ALU.mult, ALU.add, inc="dve")
            dve("scalar_tensor_tensor", cols[:, 6:7], cols[:, 2:3],
                cols[:, 2:3], cols[:, 3:4], ALU.mult,
                ALU.subtract, waits=[P.tok("dve")], inc="dve")
            act(cols[:, 4:5], cols[:, 6:7], AF.Ln, scale=-1.0,
                bias=cpool[:, 3:4], waits=[P.tok("dve")], inc="act")
            act(cols[:, 4:5], cols[:, 4:5], AF.Exp, scale=-0.5,
                waits=[P.tok("act")], inc="act")
            fl_act(cols)
            if "sp_p3d" in debug and b == 0 and c == 0:
                dma("gpsimd", sp["p3d"][1][:, 0:8], cols[:],
                    waits=[P.tok("act")], inc="dg")
                P.fence()
            dve("tensor_scalar", y1sb[:], y1pre[:], cols[:, 2:3],
                cols[:, 4:5], ALU.subtract, ALU.mult,
                waits=[P.tok("act")], inc="dve")
            fl_dve(y1sb)
            y1_tok = P.tok("dve")
            if "sp_p3d" in debug and b == 0 and c == 0:
                dve("tensor_copy", colsd[:], cols[:], inc="dve")
                dma("gpsimd", sp["p3d"][1][:, 0:8], colsd[:],
                    waits=[P.tok("dve")], inc="dg")
            y1pre_free = y1_tok
            bw = [y1_tok]
            if y1b_free[1]:
                bw.append(y1b_free)
            dve("tensor_copy", y1b[:], y1sb[:], waits=bw, inc="dve")
            fl_dve(y1b)
            y1b_tok = P.tok("dve")
            dma("gpsimd", sp["y1"][b, c], y1sb[:], waits=[y1_tok],
                inc="y1s")
            y1sb_free = P.tok("y1s")
            tw = [y1b_tok]
            if y1T_free[1]:
                tw.append(y1T_free)
            dmat(y1T[:].rearrange("p (kd j) -> p kd j", kd=16),
                 y1b[:].rearrange("p (kd j) -> p kd j", kd=16),
                 waits=tw, inc="ogt")
            y1b_free = P.tok("ogt")
            dma("gpsimd", sp["y1T"][b, c].rearrange("kd p j -> p kd j"),
                y1T[:].rearrange("p (kd j) -> p kd j", kd=16),
                waits=[P.tok("ogt")], inc="y1ts")
            y1T_free = P.tok("y1ts")
            it += 1
    while w3j["i"] < len(w3jobs):
        w3_step(6)
    st.close()


def phase_p4(nc, P, io, sp, ostack, debug, env):
    mm, ptrans, dve, gps, act, dma, dmat = mk_helpers(P)
    sb_in, ps_in = env["sb_in"], env["ps_in"]
    cpool = env["cpool"]
    dumd, duma = env["dumd"], env["duma"]
    thrc, nthrc = env["thrc"], env["nthrc"]

    def fl_dve(src):
        dve("tensor_copy", dumd[0:1, 0:1], src[0:1, 0:1], inc="dve")

    def fl_act(src):
        act(duma[0:1, 0:1], src[0:1, 0:1], AF.Copy, inc="act")

    st = contextlib.ExitStack()
    y1T4 = sb_in(st, "y1T4", [128, KD * 4096], BF16)
    w1buf = sb_in(st, "w1buf", [128, 2048])
    w2buf = sb_in(st, "w2buf", [128, 2048])
    w1q = sb_in(st, "w1q", [128, 2048], BF16)
    w2q = sb_in(st, "w2q", [128, 2048], BF16)
    ltb = sb_in(st, "ltb", [128, 2048])
    ltb2 = sb_in(st, "ltb2", [128, 2048])
    sigb = sb_in(st, "sigb", [128, 512])
    hidb = [sb_in(st, f"hidb{i}", [128, 512], BF16) for i in range(2)]
    px1 = ps_in(st, "px1", [128, 512])
    px2 = ps_in(st, "px2", [128, 512])

    for kd in range(KD):
        for bb in range(B):
            off = kd * 4096 + bb * 2048
            dma("sync",
                y1T4[:, off:off + 2048]
                .rearrange("p (c j) -> p c j", c=NCH),
                sp["y1T"][bb, :, kd].transpose([1, 0, 2]), inc="y1tl")
    y1t_tok = P.tok("y1tl")

    w_tok = (None, 0)
    wq_free = (None, 0)
    hid_free = [(None, 0), (None, 0)]
    sig_free = (None, 0)
    px_free = [(None, 0), (None, 0)]
    for hb in range(HB):
        w8 = [wq_free] if wq_free[1] else []
        dma("sync", w1buf[:],
            io["w1t"][hb].rearrange("p kd c -> p (kd c)"),
            waits=w8, inc="w13")
        dma("sync", w2buf[:],
            io["w2t"][hb].rearrange("p kd c -> p (kd c)"), inc="w13")
        w_tok = P.tok("w13")
        qw = [w_tok, env["thr_tok"]]
        dve("tensor_scalar", ltb[:], w1buf[:], nthrc[:, 0:1], 0.0,
            ALU.is_lt, ALU.add, waits=qw, inc="dve")
        dve("scalar_tensor_tensor", w1q[:], w1buf[:], thrc[:, 0:1],
            ltb[:], ALU.is_gt, ALU.subtract, inc="dve")
        dve("tensor_scalar", ltb2[:], w2buf[:], nthrc[:, 1:2], 0.0,
            ALU.is_lt, ALU.add, inc="dve")
        dve("scalar_tensor_tensor", w2q[:], w2buf[:], thrc[:, 1:2],
            ltb2[:], ALU.is_gt, ALU.subtract, inc="dve")
        fl_dve(w2q)
        q_tok = P.tok("dve")
        wq_free = q_tok
        if "sp_w1qd" in debug:
            dma("gpsimd", sp["w1qd"][hb, 0], w1q[:], waits=[q_tok],
                inc="dg")
            dma("gpsimd", sp["w1qd"][hb, 1], w2q[:], waits=[q_tok],
                inc="dg")
            P.fence()
        for tb in range(8):
            hpr = tb % 2
            m1 = [q_tok, y1t_tok]
            if px_free[0][1]:
                m1.append(px_free[0])
            for kd in range(KD):
                mm(px1[:], w1q[:, kd * 128:(kd + 1) * 128],
                   y1T4[:, kd * 4096 + tb * 512:kd * 4096 + (tb + 1) * 512],
                   kd == 0, kd == KD - 1,
                   waits=m1 if kd == 0 else [],
                   inc="pe" if kd == KD - 1 else None)
            m2 = [px_free[1]] if px_free[1][1] else []
            for kd in range(KD):
                mm(px2[:], w2q[:, kd * 128:(kd + 1) * 128],
                   y1T4[:, kd * 4096 + tb * 512:kd * 4096 + (tb + 1) * 512],
                   kd == 0, kd == KD - 1,
                   waits=m2 if kd == 0 else [],
                   inc="pe" if kd == KD - 1 else None)
            mm2 = P.tok("pe")
            sw = [mm2]
            if sig_free[1]:
                sw.append(sig_free)
            act(sigb[:], px1[:], AF.Sigmoid, waits=sw, inc="act")
            fl_act(sigb)
            hw = [P.tok("act")]
            if hid_free[hpr][1]:
                hw.append(hid_free[hpr])
            dve("tensor_mul", hidb[hpr][:], sigb[:], px2[:],
                waits=hw, inc="dve")
            fl_dve(hidb[hpr])
            px_free[0] = P.tok("dve")
            px_free[1] = px_free[0]
            sig_free = px_free[0]
            sname = "hds" if hpr == 0 else "hdl"
            dma("gpsimd", sp["hid"][hb, tb], hidb[hpr][:],
                waits=[P.tok("dve")], inc=sname)
            hid_free[hpr] = P.tok(sname)
    st.close()
    P.barrier()

    st = contextlib.ExitStack()
    w3h = [sb_in(st, f"w3h{i}", [128, 22 * 512], BF16) for i in range(2)]
    hsl = sb_in(st, "hsl", [128, HB * 512], BF16)
    y2pre = sb_in(st, "y2pre", [128, 4 * 2048])
    y1ld = sb_in(st, "y1ld", [128, 4 * 2048])
    outb = sb_in(st, "outb", [128, 2048])
    cols = sb_in(st, "colsb", [128, 8])
    py2 = [ps_in(st, f"py2_{i}", [128, 512]) for i in range(4)]

    w3h_tok = [(None, 0), (None, 0)]
    w3h_free = [(None, 0), (None, 0)]
    h_tok = (None, 0)
    h_free = (None, 0)
    y1_tok = (None, 0)
    y1_free = (None, 0)
    py_free = [(None, 0)] * 4
    y2_free = (None, 0)
    out_free = (None, 0)

    for tb in range(8):
        b = tb // 4
        t0 = (tb % 4) * 512
        hw = [h_free] if h_free[1] else []
        dma("sync", hsl[:].rearrange("p (h c) -> p h c", h=HB),
            sp["hid"][:, tb].transpose([1, 0, 2]),
            waits=hw + [P.tok("hds"), P.tok("hdl")], inc="y1tl")
        h_tok = P.tok("y1tl")
        yw = [y1_free] if y1_free[1] else []
        for ts_ in range(4):
            cc = (t0 + ts_ * 128) // 128
            dma("scalar", y1ld[:, ts_ * 2048:(ts_ + 1) * 2048],
                sp["y1"][b, cc], waits=yw if ts_ == 0 else [], inc="y1l")
        y1_tok = P.tok("y1l")
        for db in range(4):
            for half in range(2):
                par = half
                sname = "w3l" if par == 0 else "w3ql"
                w8 = [w3h_free[par]] if w3h_free[par][1] else []
                dma("sync",
                    w3h[par][:].rearrange("p (h c) -> p h c", h=22),
                    sp["w3q"][db, half * 22:half * 22 + 22]
                    .transpose([1, 0, 2]), waits=w8, inc=sname)
                w3h_tok[par] = P.tok(sname)
                for ts_ in range(4):
                    mw = [w3h_tok[par], h_tok]
                    if half == 0 and py_free[ts_][1]:
                        mw.append(py_free[ts_])
                    for kh in range(22):
                        mm(py2[ts_][:],
                           hsl[:, (half * 22 + kh) * 512 + ts_ * 128:
                               (half * 22 + kh) * 512 + ts_ * 128 + 128],
                           w3h[par][:, kh * 512:(kh + 1) * 512],
                           half == 0 and kh == 0,
                           half == 1 and kh == 21,
                           waits=mw if kh == 0 else [],
                           inc="pe" if kh == 21 else None)
                w3h_free[par] = P.tok("pe")
            e_mm = P.tok("pe")
            for ts_ in range(4):
                ew = [e_mm, y1_tok]
                if y2_free[1] and db == 0 and ts_ == 0:
                    ew.append(y2_free)
                dve("tensor_add",
                    y2pre[:, ts_ * 2048 + db * 512:
                          ts_ * 2048 + (db + 1) * 512],
                    py2[ts_][:],
                    y1ld[:, ts_ * 2048 + db * 512:
                         ts_ * 2048 + (db + 1) * 512],
                    waits=ew, inc="dve")
                py_free[ts_] = P.tok("dve")
        fl_dve(y2pre)
        pre_tok = P.tok("dve")
        y1_free = pre_tok
        h_free = P.tok("pe")
        for ts_ in range(4):
            row = slice(ts_ * 2048, (ts_ + 1) * 2048)
            dve("tensor_reduce", cols[:, 0:1], y2pre[:, row], AX.X,
                ALU.add, waits=[pre_tok], inc="dve")
            fl_dve(cols)
            dve("memset", cols[:, 1:2], 0.0, inc="dve")
            fl_dve(cols)
            ow = [P.tok("dve")]
            if out_free[1]:
                ow.append(out_free)
            act(outb[:], y2pre[:, row], AF.Square,
                accum_out=cols[:, 1:2], waits=ow, inc="act")
            fl_act(cols)
            dve("tensor_scalar", cols[:, 2:3], cols[:, 0:1], 1.0 / D,
                0.0, ALU.mult, ALU.add, waits=[P.tok("act")], inc="dve")
            dve("tensor_scalar", cols[:, 3:4], cols[:, 1:2], 1.0 / D,
                0.0, ALU.mult, ALU.add, inc="dve")
            dve("scalar_tensor_tensor", cols[:, 6:7], cols[:, 2:3],
                cols[:, 2:3], cols[:, 3:4], ALU.mult,
                ALU.subtract, waits=[P.tok("dve")], inc="dve")
            act(cols[:, 4:5], cols[:, 6:7], AF.Ln, scale=-1.0,
                bias=cpool[:, 3:4], waits=[P.tok("dve")], inc="act")
            act(cols[:, 4:5], cols[:, 4:5], AF.Exp, scale=-0.5,
                waits=[P.tok("act")], inc="act")
            fl_act(cols)
            dve("tensor_scalar", outb[:], y2pre[:, row], cols[:, 2:3],
                cols[:, 4:5], ALU.subtract, ALU.mult,
                waits=[P.tok("act")], inc="dve")
            fl_dve(outb)
            dma("gpsimd",
                io["out"][b, t0 + ts_ * 128:t0 + (ts_ + 1) * 128, :],
                outb[:], waits=[P.tok("dve")], inc="dx")
            out_free = P.tok("dx")
        y2_free = P.tok("dx")
    st.close()


def host_layout(inputs):
    x = np.ascontiguousarray(
        np.asarray(inputs["L0_attn_hidden_states"], np.float32))
    wqkv = np.asarray(inputs["L0_attn_W_qkv"], np.float32)
    wz = np.asarray(inputs["L0_attn_W_z"], np.float32)
    wa = np.asarray(inputs["L0_attn_W_a"], np.float32)
    wb = np.asarray(inputs["L0_attn_W_b"], np.float32)
    convw = np.asarray(inputs["L0_attn_conv_w"], np.float32)
    dtbv = np.asarray(inputs["L0_attn_dt_bias"], np.float32)
    alogv = np.asarray(inputs["L0_attn_A_log"], np.float32)
    normw = np.asarray(inputs["L0_attn_norm_w"], np.float32)
    wout = np.ascontiguousarray(np.asarray(inputs["L0_attn_W_out"],
                                           np.float32))
    w1 = np.asarray(inputs["L0_ffn_w1"], np.float32)
    w2 = np.asarray(inputs["L0_ffn_w2"], np.float32)
    w3 = np.asarray(inputs["L0_ffn_w3"], np.float32)

    com = {}
    com["wqkv"] = np.ascontiguousarray(
        wqkv.reshape(KD, 128, CB, 128).transpose(2, 1, 0, 3))
    com["wz"] = np.ascontiguousarray(
        wz.reshape(KD, 128, 8, 512).transpose(2, 0, 1, 3))
    com["wab"] = np.ascontiguousarray(
        np.concatenate([wa, wb], axis=1).reshape(KD, 128, 64))
    com["wout"] = wout
    com["convw"] = np.ascontiguousarray(
        convw.reshape(CB, 128, KC).transpose(1, 0, 2).reshape(128, CB * KC))
    com["alog"] = np.ascontiguousarray(alogv.reshape(NV, 1))
    com["dtb"] = np.ascontiguousarray(dtbv.reshape(NV, 1))
    com["normw4"] = np.ascontiguousarray(np.tile(normw, 4).reshape(1, 512))
    w1t = np.ascontiguousarray(w1.T)
    w2t = np.ascontiguousarray(w2.T)
    w3t = np.ascontiguousarray(w3.T)
    com["w1t"] = np.ascontiguousarray(
        w1t.reshape(KD, 128, HB, 128).transpose(2, 1, 0, 3))
    com["w2t"] = np.ascontiguousarray(
        w2t.reshape(KD, 128, HB, 128).transpose(2, 1, 0, 3))
    com["w3t"] = np.ascontiguousarray(
        w3t.reshape(HB, 128, 4, 512).transpose(2, 0, 1, 3))
    com["identf"] = np.eye(128, dtype=np.float32)
    com["identb"] = np.eye(128, dtype=np.float32).astype(BFNP)
    mAu = np.full((128, 128), -1e30, np.float32)
    mAu[np.triu(np.ones((128, 128), bool), 1)] = 0.0
    mM = np.full((128, 128), -1e30, np.float32)
    mM[np.triu(np.ones((128, 128), bool), 0)] = 0.0
    com["maskAu4"] = np.ascontiguousarray(np.tile(mAu, (1, 4)))
    com["maskM4"] = np.ascontiguousarray(np.tile(mM, (1, 4)))
    com["onesb"] = np.ones((128, 1), np.float32).astype(BFNP)
    com["onesf"] = np.ones((1, 128), np.float32)
    com["onescf"] = np.ones((128, 1), np.float32)
    cp = np.zeros((128, 8), np.float32)
    cp[:, 0] = 1e-6
    cp[:, 1] = 1.0
    cp[:, 2] = -HALFLN
    cp[:, 3] = 1e-5
    com["cpool"] = cp

    in_maps = []
    for core in range(NCORE):
        m = dict(com)
        xs = np.ascontiguousarray(x[2 * core:2 * core + 2])
        m["x"] = xs
        m["xT"] = np.ascontiguousarray(xs.transpose(0, 2, 1))
        in_maps.append(m)
    return in_maps


def kernel(**inputs):
    nc, io, sp = build_nc()
    in_maps = host_layout(inputs)
    res = run_bass_kernel_spmd(nc, in_maps, core_ids=list(range(NCORE)))
    out = np.concatenate([r["out"] for r in res.results], axis=0)
    return out.astype(np.float32)
